# version 02 (final)
# baseline (speedup 1.0000x reference)
import os
import math
import numpy as np
import ml_dtypes
import concourse.bass as bass
import concourse.mybir as mybir
from concourse.bass_utils import run_bass_kernel_spmd

F32 = mybir.dt.float32
BF16 = mybir.dt.bfloat16
AF = mybir.ActivationFunctionType
ALU = mybir.AluOpType
BF = ml_dtypes.bfloat16

NCORES = 8
GROUPS = [[0, 1], [2, 3], [4, 5], [6, 7]]
B, TFULL, D = 4, 4096, 1024
TOK = 2048
DC = D // 128
EPS_LN = 1e-5
ENGS = ("sync", "gpsimd", "tensor", "scalar", "vector")
NCOL = 512


class Sem:
    def __init__(self, name):
        self.name = name
        self.n = 0
        self.h = None


class _Tok:
    __slots__ = ("rec",)
    def __init__(self, rec):
        self.rec = rec
    def then_inc(self, sem, value=None):
        assert len(self.rec[3]) < 2, "too many sem updates on one instruction"
        self.rec[3].append((sem, value))
        sem.n += 1 if value is None else value
        return self


NON_COMPUTE = {"wait_ge", "dma_start", "dma_start_transpose", "collective_compute"}


class Eng:
    def __init__(self, name, auto=None):
        self.name = name
        self.calls = []
        self.auto = auto

    def __getattr__(self, op):
        if op.startswith("_"):
            raise AttributeError(op)
        def call(*args, **kwargs):
            if self.auto is not None and op not in NON_COMPUTE:
                if self.auto.n > 0:
                    self.calls.append(["wait_ge", (self.auto, self.auto.n), {}, []])
                rec = [op, args, kwargs, []]
                self.calls.append(rec)
                tok = _Tok(rec)
                tok.then_inc(self.auto, 1)
                return tok
            rec = [op, args, kwargs, []]
            self.calls.append(rec)
            return _Tok(rec)
        return call

    def wait(self, sem, val=None):
        v = sem.n if val is None else val
        if v > 0:
            self.calls.append(["wait_ge", (sem, v), {}, []])

    def replay(self, eng):
        for op, args, kwargs, incs in self.calls:
            args = tuple(a.h if isinstance(a, Sem) else a for a in args)
            inst = getattr(eng, op)(*args, **kwargs)
            for sem, value in incs:
                if value is None:
                    inst.then_inc(sem.h)
                else:
                    inst.then_inc(sem.h, value)


class Arena:
    def __init__(self, handle, cols_bf16):
        self.t = handle
        self.f32v = handle.ap().bitcast(F32)
        self.nb = cols_bf16 * 2
        self.off = 0
        self.marks = []
        self.peak = 0

    def alloc(self, dtype, cols, align=64):
        size = cols * (4 if dtype == F32 else 2)
        self.off = (self.off + align - 1) // align * align
        assert self.off + size <= self.nb, f"arena overflow {self.off}+{size}>{self.nb}"
        s = self.off
        self.off += size
        self.peak = max(self.peak, self.off)
        if dtype == BF16:
            return self.t[:, s // 2: s // 2 + cols]
        return self.f32v[:, s // 4: s // 4 + cols]

    def push(self):
        self.marks.append(self.off)

    def pop(self):
        self.off = self.marks.pop()


class KB:
    def __init__(self, nc, hp):
        self.nc = nc
        self.hp = hp
        self.sems = {}
        self.e = {n: Eng(n) for n in ("sync", "tensor")}
        self.e["vector"] = Eng("vector", self.sem("sv"))
        self.e["scalar"] = Eng("scalar", self.sem("ss"))
        self.e["gpsimd"] = Eng("gpsimd", self.sem("sg"))
        self.dram = {}
        self.mm_q = 0
        self.bank_free = [(0, 0)] * 4
        self.wslab_free = [(0, 0)] * 2
        self.cc_n = 0

    def epi_mark(self):
        return (self.sem("ss").n, self.sem("sv").n)

    def refresh_banks(self):
        m = self.epi_mark()
        self.bank_free = [m] * 4
        self.wslab_free = [m] * 2

    def wait_mark(self, eng, mark):
        eng.wait(self.sem("ss"), mark[0])
        eng.wait(self.sem("sv"), mark[1])

    def wait_dma(self, eng):
        for nm in ("sld", "sldg", "slds"):
            eng.wait(self.sem(nm))

    def sem(self, name):
        if name not in self.sems:
            self.sems[name] = Sem(name)
        return self.sems[name]

    def dram_t(self, name, shape, dtype, kind=None):
        if name not in self.dram:
            if kind:
                self.dram[name] = self.nc.dram_tensor(name, shape, dtype, kind=kind)
            else:
                self.dram[name] = self.nc.dram_tensor(name, shape, dtype)
        return self.dram[name]

    def col(self, name):
        i = self.hp.add_col_ref(name)
        return self.consts[:, i:i + 1]

    def dbg(self, name, ap, rows=128):
        if not getattr(self, "debug_on", False):
            return
        n = ap.shape[-1]
        t = self.dram_t(f"dbg_{name}", [rows, n], F32, kind="ExternalOutput")
        g = self.e["gpsimd"]
        for s in self.sems.values():
            g.wait(s)
        g.dma_start(t[:, :], ap).then_inc(self.sem("sldg"), 16)

    def collective(self, kind, op, ins, outs):
        G = self.e["gpsimd"]
        scc = self.sem("scc")
        G.collective_compute(kind, op, replica_groups=GROUPS, ins=ins, outs=outs).then_inc(scc)
        self.cc_n += 1
        for n in ENGS:
            self.e[n].wait(scc, self.cc_n)


class HostPrep:
    def __init__(self, inputs):
        self.I = inputs
        self.cols = []
        self.col_idx = {}
        self.percore = [dict() for _ in range(NCORES)]
        self.overlays = [[] for _ in range(NCORES)]
        self.shared = {}

    def add_col(self, name, vec):
        if name in self.col_idx:
            return self.col_idx[name]
        vec = np.asarray(vec, np.float32).reshape(-1)
        if vec.shape[0] < 128:
            vec = np.concatenate([vec, np.zeros(128 - vec.shape[0], np.float32)])
        assert vec.shape == (128,), f"col {name}: {vec.shape}"
        self.col_idx[name] = len(self.cols)
        self.cols.append((name, vec))
        return self.col_idx[name]

    def add_col_ref(self, name):
        assert name in self.col_idx, f"col {name} not registered"
        return self.col_idx[name]

    def add_cols(self, base, vec, pad_to=None):
        vec = np.asarray(vec, np.float32).reshape(-1)
        if pad_to is not None and vec.shape[0] < pad_to:
            vec = np.concatenate([vec, np.zeros(pad_to - vec.shape[0], np.float32)])
        n = (vec.shape[0] + 127) // 128
        for c in range(n):
            self.add_col(f"{base}_{c}", vec[c * 128:(c + 1) * 128])
        return n

    def add_percore_col(self, name, vecs):
        if name in self.col_idx:
            return self.col_idx[name]
        idx = self.add_col(name, np.zeros(128, np.float32))
        for c in range(NCORES):
            v = np.asarray(vecs[c], np.float32).reshape(-1)
            if v.shape[0] < 128:
                v = np.concatenate([v, np.zeros(128 - v.shape[0], np.float32)])
            self.overlays[c].append((idx, v))
        return idx

    def add_shared(self, name, arr):
        if name not in self.shared:
            self.shared[name] = np.ascontiguousarray(arr)

    def add_percore(self, name, arrs):
        if name not in self.percore[0]:
            for c in range(NCORES):
                self.percore[c][name] = np.ascontiguousarray(arrs[c])

    def finalize(self):
        assert len(self.cols) <= NCOL, f"too many cols {len(self.cols)}"
        consts = np.zeros((128, NCOL), np.float32)
        for i, (nm, v) in enumerate(self.cols):
            consts[:, i] = v
        maps = []
        for c in range(NCORES):
            m = dict(self.shared)
            m.update(self.percore[c])
            cc = consts
            if self.overlays[c]:
                cc = consts.copy()
                for idx, v in self.overlays[c]:
                    cc[:, idx] = v
            m["consts"] = cc
            maps.append(m)
        return maps


def _block_w(out):
    K, M = out.shape
    KC, MC = K // 128, M // 128
    b = out.reshape(KC, 128, MC, 128).transpose(1, 2, 0, 3).reshape(128, MC * KC * 128)
    return np.ascontiguousarray(b)


def prep_wT(w, K=None, M=None):
    wt = np.ascontiguousarray(np.asarray(w, np.float32).T)
    k, m = wt.shape
    K = K or (k + 127) // 128 * 128
    M = M or (m + 127) // 128 * 128
    out = np.zeros((K, M), np.float32)
    out[:k, :m] = wt
    return _block_w(out)


def prep_w(w, K=None, M=None):
    wt = np.asarray(w, np.float32)
    k, m = wt.shape
    K = K or (k + 127) // 128 * 128
    M = M or (m + 127) // 128 * 128
    out = np.zeros((K, M), np.float32)
    out[:k, :m] = wt
    return _block_w(out)


def emit_linear(kb, wap, K, M, rhs_view, ntok, epi):
    KC, MC = K // 128, M // 128
    G, T = kb.e["gpsimd"], kb.e["tensor"]
    swl, smm, sep = kb.sem("swl"), kb.sem("smm"), kb.sem("sep")
    m_per_slab = max(1, min(MC, (kb.wscr_cols // 128) // KC))
    ntile = (ntok + 511) // 512
    sl, m0 = 0, 0
    while m0 < MC:
        mn = min(m_per_slab, MC - m0)
        half = kb.wscr_half[sl % 2]
        kb.wait_mark(G, kb.wslab_free[sl % 2])
        src = wap[:, m0 * KC * 128:(m0 + mn) * KC * 128]
        dst = half[:, : mn * KC * 128]
        G.dma_start(dst, src).then_inc(swl, 16)
        T.wait(swl)
        for ml in range(mn):
            m = m0 + ml
            for n in range(ntile):
                n0 = n * 512
                nn = min(512, ntok - n0)
                q = kb.mm_q % 4
                kb.mm_q += 1
                psv = kb.ps_mm[:, q * 512: q * 512 + nn]
                kb.wait_mark(T, kb.bank_free[q])
                for kc in range(KC):
                    inst = T.matmul(psv,
                                    half[:, (ml * KC + kc) * 128:(ml * KC + kc) * 128 + 128],
                                    rhs_view(kc, n0, nn),
                                    start=(kc == 0), stop=(kc == KC - 1))
                inst.then_inc(smm, 1)
                epi(m, n0, nn, psv)
                kb.bank_free[q] = kb.epi_mark()
        kb.wslab_free[sl % 2] = kb.epi_mark()
        m0 += mn
        sl += 1


def epi_act(kb, func, out_fn, bias_fn=None, scale_fn=None, accum_fn=None):
    S = kb.e["scalar"]
    smm, sep = kb.sem("smm"), kb.sem("sep")
    def epi(m, n0, nn, psv):
        S.wait(smm)
        kw = {}
        if bias_fn is not None:
            b = bias_fn(m)
            if b is not None:
                kw["bias"] = b
        if scale_fn is not None:
            s = scale_fn(m)
            if s is not None:
                kw["scale"] = s
        if accum_fn is not None:
            kw["accum_out"] = accum_fn(m, n0, nn)
        S.activation(out_fn(m, n0, nn), psv, func, **kw)
    return epi


def emit_ln(kb, xs, ubf, ntok, g_col=None, b_col=None):
    G, T, S, V, SY = (kb.e[n] for n in ("gpsimd", "tensor", "scalar", "vector", "sync"))
    smm, sep, sv, ss = kb.sem("smm"), kb.sem("sep"), kb.sem("sv"), kb.sem("ss")
    ar = kb.ar
    ar.push()
    ntile = ntok // 512
    sq = ar.alloc(F32, 512)
    mu = ar.alloc(F32, ntok)[0:1, :]
    r2 = ar.alloc(F32, ntok)[0:1, :]
    brow = ar.alloc(F32, ntok)[0:1, :]
    tmp = ar.alloc(F32, 512)
    oc = kb.ones_col
    for n in range(ntile):
        nsl = slice(n * 512, (n + 1) * 512)
        for c in range(DC):
            S.wait(smm)
            S.activation(sq, xs(c)[:, nsl], AF.Square)
            if c == 0:
                T.wait(sv)
            T.matmul(kb.ps_a[0:1, 0:512], oc, xs(c)[:, nsl], start=(c == 0), stop=(c == DC - 1))
            T.wait(ss)
            inst = T.matmul(kb.ps_b[0:1, 0:512], oc, sq, start=(c == 0), stop=(c == DC - 1))
            inst.then_inc(smm, 1)
        V.wait(smm)
        V.tensor_scalar(mu[:, nsl], kb.ps_a[0:1, 0:512], 1.0 / D, None, op0=ALU.mult)
        V.tensor_scalar(r2[:, nsl], kb.ps_b[0:1, 0:512], 1.0 / D, None, op0=ALU.mult)
    V.tensor_tensor(brow, mu, mu, op=ALU.mult)
    V.tensor_tensor(r2, r2, brow, op=ALU.subtract)
    S.wait(sv)
    S.activation(r2, r2, AF.Ln, bias=kb.col("c_epsln")[0:1, :])
    S.activation(r2, r2, AF.Exp, scale=-0.5)
    V.wait(ss)
    V.scalar_tensor_tensor(brow, mu, -1.0, r2, op0=ALU.mult, op1=ALU.mult)
    T.wait(sv)
    for n in range(ntile):
        nsl = slice(n * 512, (n + 1) * 512)
        T.matmul(kb.ps_c[:, 0:512], kb.ones_row, r2[:, nsl], start=True, stop=True)
        inst = T.matmul(kb.ps_d[:, 0:512], kb.ones_row, brow[:, nsl], start=True, stop=True)
        inst.then_inc(smm, 1)
        V.wait(smm)
        for c in range(DC):
            V.tensor_tensor(tmp, xs(c)[:, nsl], kb.ps_c[:, 0:512], op=ALU.mult)
            if g_col is None:
                V.tensor_tensor(ubf(c)[:, nsl], tmp, kb.ps_d[:, 0:512], op=ALU.add)
            else:
                V.tensor_tensor(tmp, tmp, kb.ps_d[:, 0:512], op=ALU.add)
                V.tensor_scalar(ubf(c)[:, nsl], tmp, g_col(c), b_col(c), op0=ALU.mult, op1=ALU.add)
        if n < ntile - 1:
            T.wait(sv)
    ar.pop()


def emit_load_x(kb, xin, xs_buf, ntok=TOK, col0=0):
    SY = kb.e["sync"]
    sld = kb.sem("sld")
    for c in range(DC):
        SY.dma_start(xs_buf[:, c * ntok:(c + 1) * ntok],
                     xin[c * 128:(c + 1) * 128, col0:col0 + ntok]).then_inc(sld, 16)
    for n in ("tensor", "scalar", "vector", "gpsimd"):
        kb.wait_dma(kb.e[n])
    return lambda c: xs_buf[:, c * ntok:(c + 1) * ntok]


def make_residual_epi(kb, xin, xout, alpha=1.0, alpha_col_fn=None, func=None,
                      bias_fn=None, pre_scale=None, n0_off=0):
    G, T, S, V, SY = (kb.e[n] for n in ("gpsimd", "tensor", "scalar", "vector", "sync"))
    smm, sep, sv, ss, sld = (kb.sem(n) for n in ("smm", "sep", "sv", "ss", "sld"))
    ar = kb.ar
    xold = ar.alloc(F32, 2 * 512)
    xnew = ar.alloc(F32, 2 * 512)
    tband = ar.alloc(F32, 2 * 512)
    state = {"i": 0}
    def epi(m, n0, nn, psv):
        i = state["i"] % 2
        state["i"] += 1
        a0 = n0 + n0_off
        xo = xold[:, i * 512: i * 512 + nn]
        xn = xnew[:, i * 512: i * 512 + nn]
        SY.wait(sv)
        SY.dma_start(xo, xin[m * 128:(m + 1) * 128, a0:a0 + nn]).then_inc(sld, 16)
        src = psv
        if func is not None:
            S.wait(smm)
            S.wait(sv)
            kw = {}
            if bias_fn is not None:
                b = bias_fn(m)
                if b is not None:
                    kw["bias"] = b
            if pre_scale is not None:
                kw["scale"] = pre_scale
            tb = tband[:, i * 512: i * 512 + nn]
            S.activation(tb, psv, func, **kw)
            src = tb
            V.wait(ss)
        else:
            V.wait(smm)
        kb.wait_dma(V)
        sc = alpha_col_fn(m) if alpha_col_fn is not None else alpha
        V.scalar_tensor_tensor(xn, src, sc, xo, op0=ALU.mult, op1=ALU.add)
        SY.wait(sv)
        SY.dma_start(xout[m * 128:(m + 1) * 128, a0:a0 + nn], xn).then_inc(sld, 16)
    return epi


WSCR_COLS = 4096 * 2
ARENA_COLS = 83000


def kb_alloc_fixed(kb, ctx):
    E = ctx.enter_context
    nc = kb.nc
    kb.consts_t = E(nc.sbuf_tensor("sb_consts", [128, NCOL], F32))
    kb.consts = kb.consts_t[:, :]
    kb.rowc_t = E(nc.sbuf_tensor("sb_rowc", [1, 256], F32))
    kb.ones_row = kb.rowc_t[0:1, 0:128]
    kb.wscr_t = E(nc.sbuf_tensor("wscr", [128, WSCR_COLS], BF16))
    kb.wscr_cols = WSCR_COLS // 2
    kb.wscr_half = [kb.wscr_t[:, 0:kb.wscr_cols], kb.wscr_t[:, kb.wscr_cols:]]
    kb.arena_t = E(nc.sbuf_tensor("arena", [128, ARENA_COLS], BF16))
    kb.ar = Arena(kb.arena_t, ARENA_COLS)
    kb.ps_mm_t = E(nc.psum_tensor("ps_mm", [128, 2048], F32))
    kb.ps_mm = kb.ps_mm_t[:, :]
    kb.small_t = E(nc.sbuf_tensor("sb_small", [128, 32], BF16))
    kb.zt_t = E(nc.sbuf_tensor("sb_zt", [128, 512], BF16))
    kb.eye_t = E(nc.sbuf_tensor("sb_eye", [128, 128], F32))
    kb.eye_bf_t = E(nc.sbuf_tensor("sb_eyebf", [128, 128], BF16))
    kb.ps_a_t = E(nc.psum_tensor("ps_a", [128, 512], F32)); kb.ps_a = kb.ps_a_t[:, :]
    kb.ps_b_t = E(nc.psum_tensor("ps_b", [128, 512], F32)); kb.ps_b = kb.ps_b_t[:, :]
    kb.ps_c_t = E(nc.psum_tensor("ps_c", [128, 512], F32)); kb.ps_c = kb.ps_c_t[:, :]
    kb.ps_d_t = E(nc.psum_tensor("ps_d", [128, 512], F32)); kb.ps_d = kb.ps_d_t[:, :]


def kb_prologue(kb):
    SY = kb.e["sync"]
    sld = kb.sem("sld")
    kb.hp.add_col("ones", np.ones(128, np.float32))
    kb.hp.add_col("c_epsln", np.full(128, EPS_LN, np.float32))
    kb.hp.add_col("c_one", np.ones(128, np.float32))
    kb.hp.add_col("c_eps6", np.full(128, 1e-6, np.float32))
    kb.hp.add_shared("row_consts", np.concatenate(
        [np.ones((1, 128), np.float32), -np.ones((1, 128), np.float32)], axis=1))
    cdr = kb.dram_t("consts", [128, NCOL], F32, kind="ExternalInput")
    rdr = kb.dram_t("row_consts", [1, 256], F32, kind="ExternalInput")
    SY.dma_start(kb.consts, cdr[:, :]).then_inc(sld, 16)
    SY.dma_start(kb.rowc_t[:, :], rdr[:, :]).then_inc(sld, 16)
    kb.ones_col = kb.col("ones")
    kb.hp.add_shared("eye_f32", np.eye(128, dtype=np.float32))
    kb.hp.add_shared("eye_bf", np.eye(128, dtype=BF))
    edr = kb.dram_t("eye_f32", [128, 128], F32, kind="ExternalInput")
    ebdr = kb.dram_t("eye_bf", [128, 128], BF16, kind="ExternalInput")
    SY.dma_start(kb.eye_t[:, :], edr[:, :]).then_inc(sld, 16)
    SY.dma_start(kb.eye_bf_t[:, :], ebdr[:, :]).then_inc(sld, 16)
    kb.eye_bf = kb.eye_bf_t[:, :]
    V = kb.e["vector"]
    V.memset(kb.small_t[:, 0:1], 1.0)
    V.memset(kb.small_t[0:1, 2:18], -1.0)
    V.memset(kb.zt_t[:, :], 0.0)
    kb.zt_bf = kb.zt_t[:, :]
    kb.ones_col_bf = kb.small_t[:, 0:1]
    kb.neg_row_bf = None
    kb.neg_ones_row = kb.rowc_t[0:1, 128:256]
    for n in ENGS:
        kb.wait_dma(kb.e[n])
        kb.e[n].wait(kb.sem("sv"))


def kb_finish(kb):
    for n in ENGS:
        e = kb.e[n]
        for s in kb.sems.values():
            e.wait(s)


def kb_build_and_run(kb, percore_inputs, trace=False):
    from contextlib import ExitStack
    nc = kb.nc
    with ExitStack() as ctx:
        for s in kb.sems.values():
            s.h = ctx.enter_context(nc.semaphore(s.name))
        blk = ctx.enter_context(nc.Block())
        for name in ENGS:
            er = kb.e[name]
            def mk(er):
                def body(eng):
                    er.replay(eng)
                return body
            getattr(blk, name)(mk(er))
    maps = kb.hp.finalize()
    declared = set()
    for alloc in nc.m.functions[0].allocations:
        try:
            if alloc.kind == "ExternalInput":
                declared.add(alloc.memorylocations[0].name)
        except Exception:
            pass
    if nc.partition_id_tensor is not None:
        declared.discard(nc.partition_id_tensor.name)
    in_maps = []
    for c in range(NCORES):
        m = dict(maps[c])
        m.update(percore_inputs[c])
        m = {k: v for k, v in m.items() if k in declared}
        missing = declared - set(m)
        assert not missing, f"missing inputs: {missing}"
        in_maps.append(m)
    return run_bass_kernel_spmd(nc, in_maps, core_ids=list(range(NCORES)), trace=trace)


def new_kb_full(hp, detect_races=True):
    from contextlib import ExitStack
    nc = bass.Bass("TRN2", target_bir_lowering=False, debug=False,
                   detect_race_conditions=detect_races)
    kb = KB(nc, hp)
    ctx = ExitStack()
    kb._ctx = ctx
    kb_alloc_fixed(kb, ctx)
    kb_prologue(kb)
    return kb


def build_l0_ffn(kb, xin, xout):
    hp = kb.hp
    hp.add_shared("l0f_w1", prep_w(hp.I["L0_ffn_weight"]))
    hp.add_shared("l0f_a1", prep_w(hp.I["L0_ffn_aux_weight1"]))
    hp.add_shared("l0f_a2", prep_w(hp.I["L0_ffn_aux_weight2"]))
    hp.add_cols("l0f_b", hp.I["L0_ffn_bias"])
    w1 = kb.dram_t("l0f_w1", [128, 8192], F32, kind="ExternalInput")
    a1 = kb.dram_t("l0f_a1", [128, 8192], F32, kind="ExternalInput")
    a2 = kb.dram_t("l0f_a2", [128, 8192], F32, kind="ExternalInput")
    ar = kb.ar
    ar.push()
    u = ar.alloc(BF16, DC * TOK)
    res_epi = make_residual_epi(kb, xin, xout, alpha=0.5, func=AF.Tanh,
                                bias_fn=lambda m: kb.col(f"l0f_b_{m}"))
    ar.push()
    xs_buf = ar.alloc(F32, DC * TOK)
    xsv = emit_load_x(kb, xin, xs_buf)
    emit_ln(kb, xsv, lambda c: u[:, c * TOK:(c + 1) * TOK], TOK)
    ar.pop()
    bufA = ar.alloc(BF16, DC * TOK)
    bufB = ar.alloc(BF16, DC * TOK)
    kb.e["tensor"].wait(kb.sem("sv"))
    uview = lambda kc, n0, nn: u[:, kc * TOK + n0: kc * TOK + n0 + nn]
    Aview = lambda kc, n0, nn: bufA[:, kc * TOK + n0: kc * TOK + n0 + nn]
    Bview = lambda kc, n0, nn: bufB[:, kc * TOK + n0: kc * TOK + n0 + nn]
    emit_linear(kb, w1.ap(), 1024, 1024, uview, TOK,
                epi_act(kb, AF.Copy, lambda m, n0, nn: bufA[:, m * TOK + n0: m * TOK + n0 + nn]))
    kb.e["tensor"].wait(kb.sem("ss")); kb.e["tensor"].wait(kb.sem("sv"))
    emit_linear(kb, a1.ap(), 1024, 1024, Aview, TOK,
                epi_act(kb, AF.Copy, lambda m, n0, nn: bufB[:, m * TOK + n0: m * TOK + n0 + nn]))
    kb.e["tensor"].wait(kb.sem("ss")); kb.e["tensor"].wait(kb.sem("sv"))
    emit_linear(kb, a2.ap(), 1024, 1024, Bview, TOK, res_epi)
    ar.pop()


def build_ternary(kb, xin, xout, pfx, tag):
    hp = kb.hp
    I = hp.I
    hp.add_shared(f"{tag}w1", prep_wT(I[pfx + "w1_w"]))
    hp.add_shared(f"{tag}w2", prep_wT(I[pfx + "w2_w"]))
    hp.add_shared(f"{tag}h1", prep_wT(I[pfx + "h1_w"]))
    hp.add_shared(f"{tag}h2", prep_wT(I[pfx + "h2_w"]))
    hp.add_shared(f"{tag}w3", prep_wT(I[pfx + "w3_w"]))
    hp.add_cols(f"{tag}s1", I[pfx + "w1_s"])
    hp.add_cols(f"{tag}s2", I[pfx + "w2_s"])
    hp.add_cols(f"{tag}sh1", I[pfx + "h1_s"])
    hp.add_cols(f"{tag}sh2", I[pfx + "h2_s"])
    hp.add_cols(f"{tag}s3h", 0.5 * np.asarray(I[pfx + "w3_s"], np.float32))
    w1 = kb.dram_t(f"{tag}w1", [128, 32768], F32, kind="ExternalInput")
    w2 = kb.dram_t(f"{tag}w2", [128, 32768], F32, kind="ExternalInput")
    h1w = kb.dram_t(f"{tag}h1", [128, 131072], F32, kind="ExternalInput")
    h2w = kb.dram_t(f"{tag}h2", [128, 131072], F32, kind="ExternalInput")
    w3 = kb.dram_t(f"{tag}w3", [128, 32768], F32, kind="ExternalInput")
    V, T = kb.e["vector"], kb.e["tensor"]
    smm, sep, sv, ss = kb.sem("smm"), kb.sem("sep"), kb.sem("sv"), kb.sem("ss")
    HT = 1024
    ar = kb.ar
    for th in range(2):
        t0 = th * HT
        ar.push()
        u = ar.alloc(BF16, DC * HT)
        res_epi = make_residual_epi(kb, xin, xout,
                                    alpha_col_fn=lambda m: kb.col(f"{tag}s3h_{m}"),
                                    n0_off=t0)
        ar.push()
        xs_buf = ar.alloc(F32, DC * HT)
        xsv = emit_load_x(kb, xin, xs_buf, ntok=HT, col0=t0)
        emit_ln(kb, xsv, lambda c: u[:, c * HT:(c + 1) * HT], HT)
        ar.pop()
        T.wait(sv)
        bufA = ar.alloc(BF16, 32 * HT)
        bufB = ar.alloc(BF16, 32 * HT)
        uv = lambda kc, n0, nn: u[:, kc * HT + n0: kc * HT + n0 + nn]
        Av = lambda kc, n0, nn: bufA[:, kc * HT + n0: kc * HT + n0 + nn]
        Bv = lambda kc, n0, nn: bufB[:, kc * HT + n0: kc * HT + n0 + nn]
        emit_linear(kb, w1.ap(), 1024, 4096, uv, HT,
                    epi_act(kb, AF.Silu, lambda m, n0, nn: bufA[:, m * HT + n0: m * HT + n0 + nn],
                            scale_fn=lambda m: kb.col(f"{tag}s1_{m}")))
        T.wait(ss); T.wait(sv)

        def epi_h(m, n0, nn, psv):
            V.wait(smm)
            V.wait(ss)
            V.scalar_tensor_tensor(bufB[:, m * HT + n0: m * HT + n0 + nn], psv,
                                   kb.col(f"{tag}s2_{m}"),
                                   bufA[:, m * HT + n0: m * HT + n0 + nn],
                                   op0=ALU.mult, op1=ALU.mult)
        emit_linear(kb, w2.ap(), 1024, 4096, uv, HT, epi_h)
        T.wait(ss); T.wait(sv)
        emit_linear(kb, h1w.ap(), 4096, 4096, Bv, HT,
                    epi_act(kb, AF.Gelu, lambda m, n0, nn: bufA[:, m * HT + n0: m * HT + n0 + nn],
                            scale_fn=lambda m: kb.col(f"{tag}sh1_{m}")))
        T.wait(ss); T.wait(sv)
        emit_linear(kb, h2w.ap(), 4096, 4096, Av, HT,
                    epi_act(kb, AF.Gelu, lambda m, n0, nn: bufB[:, m * HT + n0: m * HT + n0 + nn],
                            scale_fn=lambda m: kb.col(f"{tag}sh2_{m}")))
        T.wait(ss); T.wait(sv)
        emit_linear(kb, w3.ap(), 4096, 1024, Bv, HT, res_epi)
        ar.pop()


def shard_x(x):
    out = []
    for c in range(NCORES):
        b, h = c // 2, c % 2
        out.append(np.ascontiguousarray(x[b, h * TOK:(h + 1) * TOK, :].T))
    return out


def unshard_x(shards):
    x = np.empty((B, TFULL, D), np.float32)
    for c in range(NCORES):
        b, h = c // 2, c % 2
        x[b, h * TOK:(h + 1) * TOK, :] = np.asarray(shards[c]).T
    return x


def run_one_sublayer(builder, inputs, xs_percore):
    hp = HostPrep(inputs)
    kb = new_kb_full(hp)
    xin = kb.dram_t("x_in", [1024, TOK], F32, kind="ExternalInput")
    xout = kb.dram_t("x_out", [1024, TOK], F32, kind="ExternalOutput")
    builder(kb, xin, xout)
    kb_finish(kb)
    percore = [{"x_in": xs_percore[c]} for c in range(NCORES)]
    res = kb_build_and_run(kb, percore)
    return [np.asarray(res.results[c]["x_out"]) for c in range(NCORES)]


def get_builders():
    gum = {}

    def l1att(kb, xi, xo):
        if "pc" not in gum:
            gum["pc"] = make_gumbel_pc()
        build_l1_attn(kb, xi, xo, gum["pc"])
    return [
        ("L0f_a", build_l0_ffn),
        ("L0att", build_l0_attn),
        ("L0f_b", build_l0_ffn),
        ("L1f_a", build_l1_ffn),
        ("L1att", l1att),
        ("L1f_b", build_l1_ffn),
        ("L2f_a", lambda kb, xi, xo: build_ternary(kb, xi, xo, "L2_ffn_", "t2_")),
        ("L2att", build_l2_attn),
        ("L2f_b", lambda kb, xi, xo: build_ternary(kb, xi, xo, "L2_ffn_", "t2_")),
        ("L3f_a", lambda kb, xi, xo: build_ternary(kb, xi, xo, "L3_ffn_", "t3_")),
        ("L3att", build_l3_attn),
        ("L3f_b", lambda kb, xi, xo: build_ternary(kb, xi, xo, "L3_ffn_", "t3_")),
    ]


def kernel(**inputs):
    xs = shard_x(np.asarray(inputs["L0_attn_x"], np.float32))
    for name, bld in get_builders():
        xs = run_one_sublayer(bld, inputs, xs)
    return unshard_x(xs)



def emit_linear_dramrhs(kb, wap, K, M, rhs_dram_tile, ntok_total, epi, rhs_dtype=BF16):
    KC = K // 128
    SY = kb.e["sync"]
    sld, smm = kb.sem("sld"), kb.sem("smm")
    ar = kb.ar
    ar.push()
    slab = ar.alloc(rhs_dtype, KC * 512)
    for n0 in range(0, ntok_total, 512):
        SY.wait(smm)
        for kc in range(KC):
            SY.dma_start(slab[:, kc * 512:(kc + 1) * 512], rhs_dram_tile(kc, n0)).then_inc(sld, 16)
        kb.wait_dma(kb.e["tensor"])
        def epi2(m, nn0, nn, psv, _n0=n0):
            epi(m, _n0 + nn0, nn, psv)
        emit_linear(kb, wap, K, M, lambda kc, nn0, nn: slab[:, kc * 512 + nn0: kc * 512 + nn0 + nn],
                    512, epi2)
    ar.pop()


def build_l3_attn(kb, xin, xout):
    hp = kb.hp
    I = hp.I
    G, T, S, V, SY = (kb.e[n] for n in ("gpsimd", "tensor", "scalar", "vector", "sync"))
    smm, sep, sv, ss, sld = (kb.sem(n) for n in ("smm", "sep", "sv", "ss", "sld"))
    H = 8
    Wa = np.asarray(I["L3_attn_W_attn"], np.float32)
    ba = np.asarray(I["L3_attn_b_attn"], np.float32)
    wqkv_pc, wpr_pc, bp_pc, jt_pc = [], [], [], []
    start = 2.0 ** (-2.0 ** (-(math.log2(16) - 3)))
    slopes = np.asarray([start * start ** i for i in range(16)], np.float32)
    Wp = np.asarray(I["L3_attn_W_proj"], np.float32)
    for c in range(NCORES):
        hh = (c % 2) * 8
        cols = slice(hh * 64, hh * 64 + 512)
        wqkv_pc.append(_block_w(np.concatenate(
            [Wa[:, 0:1024][:, cols], Wa[:, 1024:2048][:, cols], Wa[:, 2048:3072][:, cols]], 1)))
        wpr_pc.append(np.ascontiguousarray(Wp[hh * 64: hh * 64 + 512, :].reshape(4, 128, 1024).transpose(1, 0, 2).reshape(128, 4096)))
        bp = np.asarray(I["L3_attn_b_proj"], np.float32).reshape(1, 1024)
        bp_pc.append(bp if c % 2 == 0 else np.zeros((1, 1024), np.float32))
        jt = np.zeros((128, 256), np.float32)
        for h in range(8):
            for kc in range(32):
                jt[:, h * 32 + kc] = slopes[hh + h] * (kc * 128 + np.arange(128))
        jt_pc.append(jt)
    hp.add_percore("l3_wqkv", wqkv_pc)
    hp.add_percore("l3_wpr", wpr_pc)
    hp.add_percore("l3_bp", bp_pc)
    hp.add_percore("l3_jt", jt_pc)
    for m in range(12):
        vals = []
        for c in range(NCORES):
            hh = (c % 2) * 8
            j = m // 4
            sub = ba[j * 1024:(j + 1) * 1024][hh * 64: hh * 64 + 512]
            vals.append(sub[(m % 4) * 128:(m % 4) * 128 + 128])
        hp.add_percore_col(f"l3_b_{m}", vals)
    hp.add_percore("l3_temp", [np.asarray(I["L3_attn_temperature"], np.float32).reshape(1, 1)
                               for _ in range(NCORES)])
    dm = np.zeros((128, 4 * 512), np.float32)
    for d in range(4):
        p = np.arange(128)[:, None]
        j = np.arange(512)[None, :]
        dm[:, d * 512:(d + 1) * 512] = np.where(p + d * 128 <= j, 0.0, -30000.0)
    hp.add_shared("l3_diagmask", dm.astype(BF))
    qs_pc = []
    for c in range(NCORES):
        hh = (c % 2) * 8
        qs = np.zeros((1, 8 * 4096), np.float32)
        for h in range(8):
            qs[0, h * 4096:(h + 1) * 4096] = slopes[hh + h] * np.arange(4096)
        qs_pc.append(qs.astype(BF))
    hp.add_percore("l3_qs", qs_pc)
    wqkv = kb.dram_t("l3_wqkv", [128, 1024 * 1536 // 128], F32, kind="ExternalInput")
    wpr_d = kb.dram_t("l3_wpr", [128, 4096], F32, kind="ExternalInput")
    bp_d = kb.dram_t("l3_bp", [1, 1024], F32, kind="ExternalInput")
    jt_d = kb.dram_t("l3_jt", [128, 256], F32, kind="ExternalInput")
    tmp_d = kb.dram_t("l3_temp", [1, 1], F32, kind="ExternalInput")
    dm_d = kb.dram_t("l3_diagmask", [128, 2048], BF16, kind="ExternalInput")
    qs_d = kb.dram_t("l3_qs", [1, 8 * 4096], BF16, kind="ExternalInput")
    hid_d = kb.dram_t("l3_hid", [1024, TOK], BF16)
    agh_d = kb.dram_t("l3_agh", [2048, TOK], BF16)
    vdr_d = kb.dram_t("l3_vdr", [512, TFULL], BF16)
    part_d = kb.dram_t("l3_part", [TFULL, 1024], BF16)
    rs_d = kb.dram_t("l3_rs", [TOK, 1024], BF16)

    ar = kb.ar
    ar.push()
    ar.push()
    u = ar.alloc(BF16, DC * TOK)
    xs_buf = ar.alloc(F32, DC * TOK)
    xsv = emit_load_x(kb, xin, xs_buf)
    emit_ln(kb, xsv, lambda c: u[:, c * TOK:(c + 1) * TOK], TOK)
    G.wait(sv)
    for c in range(DC):
        G.dma_start(hid_d[c * 128:(c + 1) * 128, :], u[:, c * TOK:(c + 1) * TOK]).then_inc(kb.sem("sldg"), 16)
    kb.wait_dma(G)
    ar.pop()
    kb.collective("AllGather", ALU.bypass, [hid_d.ap().opt()], [agh_d.ap().opt()])

    obuf = ar.alloc(BF16, 4 * TFULL)
    ar.push()
    srow = ar.alloc(F32, 16)[0:1, 0:1]
    srow2 = ar.alloc(F32, 16)[0:1, 0:1]
    srow3 = ar.alloc(F32, 16)[0:1, 0:1]
    s_col = ar.alloc(F32, 16)[:, 0:1]
    jt_sb = ar.alloc(F32, 256)
    SY.dma_start(jt_sb, jt_d[:, :]).then_inc(sld, 16)
    SY.dma_start(srow, tmp_d[0:1, 0:1]).then_inc(sld, 16)
    kb.wait_dma(V)
    V.tensor_scalar(srow2, srow, 8.0, None, op0=ALU.mult)
    V.wait(sv)
    V.reciprocal(srow3, srow2)
    T.wait(sv)
    T.matmul(kb.ps_c[:, 0:1], kb.ones_row, srow3, start=True, stop=True).then_inc(smm, 1)
    S.wait(smm)
    S.copy(s_col, kb.ps_c[:, 0:1])
    for e in (V, T, G):
        e.wait(ss)

    qbuf = ar.alloc(BF16, TFULL)
    kbuf = ar.alloc(BF16, TFULL)
    vstage = ar.alloc(BF16, 512)
    KCq = 8

    def rhs_tile(kc, n0):
        if n0 < TOK:
            return agh_d[kc * 128:(kc + 1) * 128, n0:n0 + 512]
        return agh_d[1024 + kc * 128: 1024 + (kc + 1) * 128, n0 - TOK:n0 - TOK + 512]

    def wslice(m0, mn):
        return wqkv.ap()[:, m0 * KCq * 128:(m0 + mn) * KCq * 128]

    def epi_v(m, n0, nn, psv):
        mm = m + 8
        V.wait(smm)
        kb.wait_dma(V)
        V.tensor_scalar(vstage[:, 0:nn], psv, kb.col(f"l3_b_{mm}"), None,
                        op0=ALU.add)
        SY.wait(sv)
        SY.dma_start(vdr_d[m * 128:(m + 1) * 128, n0:n0 + nn], vstage[:, 0:nn]).then_inc(sld, 16)
    emit_linear_dramrhs(kb, wslice(8, 4), 1024, 512, rhs_tile, TFULL, epi_v)
    vtm = ar.alloc(BF16, 32 * 512)
    kb.wait_dma(S)
    S.wait(sv)
    for tc in range(32):
        S.dma_start(vtm[:, tc * 512:(tc + 1) * 512],
                    vdr_d[:, tc * 128:(tc + 1) * 128], transpose=True).then_inc(kb.sem("slds"), 16)
    for e in (T, V, G, SY):
        kb.wait_dma(e)

    dmask = ar.alloc(BF16, 2048)
    qsl = ar.alloc(BF16, 4096)[0:1, :]
    mrow_f32 = ar.alloc(BF16, TFULL)[0:1, :]
    SY.dma_start(dmask, dm_d[:, :]).then_inc(sld, 16)
    for e in (V,):
        kb.wait_dma(e)
    mbuf = ar.alloc(F32, 32)
    mtmp = ar.alloc(F32, 2)
    mrow_buf = ar.alloc(BF16, TFULL)
    mrow_flat = mrow_buf[0:1, :]
    negrow_buf = ar.alloc(BF16, 128)
    negrow_bf = negrow_buf[0:1, :]
    e_t = ar.alloc(BF16, 2 * 512)
    sc_f32 = ar.alloc(F32, 2 * 512)
    rb = ar.alloc(F32, 512)[0:64, :]
    rrow = ar.alloc(F32, 512)[0:1, :]
    V.memset(negrow_bf, -1.0)
    V.memset(negrow_buf[64:65, :], -1.0)
    for e in (T,):
        e.wait(sv)

    def epi_q(m, n0, nn, psv, pair):
        V.wait(smm)
        V.tensor_scalar(qbuf[:, n0:n0 + nn], psv,
                        kb.col(f"l3_b_{pair}"), s_col, op0=ALU.add, op1=ALU.mult)

    def epi_k(m, n0, nn, psv, pair):
        V.wait(smm)
        V.tensor_scalar(kbuf[:, n0:n0 + nn], psv,
                        kb.col(f"l3_b_{4 + pair}"), None, op0=ALU.add)

    for h in range(H):
        g = h // 2
        off = (h % 2) * 64
        if h % 2 == 0:
            for e in (T,):
                e.wait(sv)
            emit_linear_dramrhs(kb, wslice(g, 1), 1024, 128, rhs_tile, TFULL,
                                lambda m, n0, nn, psv: epi_q(m, n0, nn, psv, g))
            emit_linear_dramrhs(kb, wslice(4 + g, 1), 1024, 128, rhs_tile, TFULL,
                                lambda m, n0, nn, psv: epi_k(m, n0, nn, psv, g))
            T.wait(sv)
        qv = lambda c0, nn: qbuf[off:off + 64, c0: c0 + nn]
        kv = lambda c0, nn: kbuf[off:off + 64, c0: c0 + nn]
        SY.wait(sv)
        SY.dma_start(qsl, qs_d[0:1, h * 4096:(h + 1) * 4096]).then_inc(sld, 16)
        kb.refresh_banks()
        for qc in range(32):
            nk = (qc // 4 + 1) * 512
            nhalf = (nk + 2047) // 2048
            for hf in range(nhalf):
                k0 = hf * 2048
                kn = min(2048, nk - k0)
                T.wait(sv)
                T.wait(ss)
                nmm = (kn + 511) // 512
                for j in range(nmm):
                    kk = min(512, kn - j * 512)
                    inst = T.matmul(kb.ps_mm[:, j * 512:j * 512 + kk], qv(qc * 128, 128),
                                    kv(k0 + j * 512, kk), start=True, stop=True)
                inst.then_inc(smm, 1)
                V.wait(smm)
                dst = mbuf[:, qc:qc + 1] if nhalf == 1 else mtmp[:, hf:hf + 1]
                V.tensor_reduce(dst, kb.ps_mm[:, 0:kn], axis=mybir.AxisListType.X,
                                op=ALU.max)
            if nhalf > 1:
                V.tensor_reduce(mbuf[:, qc:qc + 1], mtmp[:, 0:nhalf], axis=mybir.AxisListType.X,
                                op=ALU.max)
        T.wait(sv)
        for qc in range(32):
            T.transpose(kb.ps_d[0:1, 0:128], mbuf[:, qc:qc + 1], kb.eye_t[:, :]).then_inc(smm, 1)
            S.wait(smm)
            S.copy(mrow_f32[:, qc * 128:(qc + 1) * 128], kb.ps_d[0:1, 0:128])
            if qc < 31:
                T.wait(ss)
        V.wait(ss)
        kb.wait_dma(V)
        V.tensor_tensor(mrow_flat, mrow_f32, qsl, op=ALU.add)
        for e in (V, T, S):
            e.wait(sv)
        if off == 64:
            G.wait(sv)
            G.dma_start(mrow_buf[64:65, :], mrow_flat).then_inc(kb.sem("sldg"), 16)
            kb.wait_dma(T)
        if h == 0:
            kb.dbg("mrow_h0", mrow_flat, rows=1)
            kb.dbg("q_h0", qbuf[:, 0:4096])
            kb.dbg("k_h0", kbuf[:, 0:4096])
        kb.refresh_banks()
        for qt in range(8):
            kts = min(32, qt * 4 + 4)
            for kc in range(kts):
                q = kb.mm_q % 2
                kb.mm_q += 1
                psv = kb.ps_mm[:, q * 512: q * 512 + 512]
                kb.wait_mark(T, kb.bank_free[q])
                T.matmul(psv, kv(kc * 128, 128), qv(qt * 512, 512), start=True, stop=False)
                inst = T.matmul(psv, negrow_buf[off:off + 1, 0:128],
                                mrow_buf[off:off + 1, qt * 512:(qt + 1) * 512], start=False, stop=True)
                inst.then_inc(smm, 1)
                S.wait(smm)
                et = e_t[:, (kc % 2) * 512:(kc % 2) * 512 + 512]
                d = kc - qt * 4
                src_exp = psv
                if d >= 0:
                    sc = sc_f32[:, (kc % 2) * 512:(kc % 2) * 512 + 512]
                    V.wait(smm)
                    V.wait(ss)
                    V.tensor_tensor(sc, psv, dmask[:, d * 512:(d + 1) * 512], op=ALU.add)
                    src_exp = sc
                    S.wait(sv)
                S.wait(sv)
                S.activation(et, src_exp, AF.Exp, bias=jt_sb[:, h * 32 + kc: h * 32 + kc + 1])
                kb.bank_free[q] = kb.epi_mark()
                T.wait(ss); T.wait(sv)
                T.matmul(kb.ps_b[0:1, 0:512], kb.ones_col_bf, et,
                         start=(kc == 0), stop=(kc == kts - 1))
                inst = T.matmul(kb.ps_a[0:64, 0:512], vtm[:, kc * 512 + h * 64: kc * 512 + h * 64 + 64],
                                et, start=(kc == 0), stop=(kc == kts - 1))
                inst.then_inc(smm, 1)
                V.wait(smm)
                V.nop()
                if h == 0 and qt == 0 and kc == 3:
                    kb.dbg("et_h0", et)
                    kb.dbg("sc_h0", sc_f32[:, (kc % 2) * 512:(kc % 2) * 512 + 512])
            V.wait(smm)
            V.reciprocal(rrow, kb.ps_b[0:1, 0:512])
            T.wait(sv)
            T.matmul(kb.ps_c[0:64, 0:512], kb.ones_row[0:1, 0:64], rrow, start=True, stop=True).then_inc(smm, 1)
            S.wait(smm)
            S.copy(rb, kb.ps_c[0:64, 0:512])
            V.wait(ss)
            V.tensor_tensor(obuf[off:off + 64, g * TFULL + qt * 512: g * TFULL + qt * 512 + 512],
                            kb.ps_a[0:64, 0:512], rb, op=ALU.mult)
            T.wait(sv)
            if h == 0 and qt == 0:
                kb.dbg("rb_h0", rb, rows=64)
    kb.dbg("obuf", obuf[:, 0:4 * TFULL])
    for e in (T, S, V, SY, G):
        e.wait(sv)
        e.wait(ss)
    ar.pop()
    wpr = ar.alloc(BF16, 4 * 1024)
    bpr = ar.alloc(F32, 1024)[0:1, :]
    G.wait(sv)
    G.dma_start(wpr, wpr_d[:, :]).then_inc(kb.sem("sldg"), 16)
    SY.dma_start(bpr, bp_d[0:1, :]).then_inc(sld, 16)
    kb.wait_dma(T)
    ptile = ar.alloc(BF16, 2 * 512)
    kb.refresh_banks()
    for tc in range(32):
        for dn in range(2):
            q = kb.mm_q % 4
            kb.mm_q += 1
            psv = kb.ps_mm[:, q * 512: q * 512 + 512]
            kb.wait_mark(T, kb.bank_free[q])
            for g in range(4):
                T.matmul(psv, obuf[:, g * TFULL + tc * 128: g * TFULL + tc * 128 + 128],
                         wpr[:, g * 1024 + dn * 512: g * 1024 + dn * 512 + 512],
                         start=(g == 0), stop=False)
            inst = T.matmul(psv, kb.ones_row[0:1, 0:128], bpr[:, dn * 512:(dn + 1) * 512],
                            start=False, stop=True)
            inst.then_inc(smm, 1)
            S.wait(smm)
            kb.wait_dma(S)
            pt = ptile[:, (tc % 2) * 512:(tc % 2) * 512 + 512]
            S.copy(pt, psv)
            kb.bank_free[q] = kb.epi_mark()
            SY.wait(ss)
            SY.dma_start(part_d[tc * 128:(tc + 1) * 128, dn * 512:(dn + 1) * 512], pt).then_inc(sld, 16)
    kb.wait_dma(G)
    kb.collective("ReduceScatter", ALU.add, [part_d.ap().opt()], [rs_d.ap().opt()])
    if getattr(kb, "debug_on", False):
        pd = kb.dram_t("dbg_part", [TFULL, 1024], F32, kind="ExternalOutput")
        rd2 = kb.dram_t("dbg_rs", [TOK, 1024], F32, kind="ExternalOutput")
        for s2 in kb.sems.values():
            G.wait(s2)
        G.dma_start(pd[:, :], part_d[:, :]).then_inc(kb.sem("sldg"), 16)
        G.dma_start(rd2[:, :], rs_d[:, :]).then_inc(kb.sem("sldg"), 16)
    rst = ar.alloc(BF16, TOK)
    xio = ar.alloc(F32, 2 * 512)
    xio2 = ar.alloc(F32, 2 * 512)
    for c in range(DC):
        S.wait(sv)
        S.dma_start(rst, rs_d[:, c * 128:(c + 1) * 128], transpose=True).then_inc(kb.sem("slds"), 16)
        kb.wait_dma(V)
        for n in range(4):
            i = n % 2
            SY.wait(sv)
            SY.dma_start(xio[:, i * 512:(i + 1) * 512],
                         xin[c * 128:(c + 1) * 128, n * 512:(n + 1) * 512]).then_inc(kb.sem("slds"), 16)
            kb.wait_dma(V)
            V.tensor_tensor(xio2[:, i * 512:(i + 1) * 512], xio[:, i * 512:(i + 1) * 512],
                            rst[:, n * 512:(n + 1) * 512], op=ALU.add)
            SY.wait(sv)
            SY.dma_start(xout[c * 128:(c + 1) * 128, n * 512:(n + 1) * 512],
                         xio2[:, i * 512:(i + 1) * 512]).then_inc(sld, 16)
    ar.pop()


def make_gumbel_pc():
    import jax
    import jax.numpy as jnp
    with jax.default_device(jax.devices("cpu")[0]):
        key = jax.random.key(42, impl="threefry2x32")
        u = jax.random.uniform(key, (B, TFULL, TFULL), jnp.float32, 1e-6, 1.0 - 1e-6)
        gum = np.asarray(-jnp.log(-jnp.log(u)))
    idx = np.arange(TFULL)
    mask = np.where(idx[None, :] < idx[:, None], 0.0, -1e9).astype(np.float32)
    out = []
    for c in range(NCORES):
        b, hh = c // 2, c % 2
        out.append(np.ascontiguousarray(gum[b, hh * TOK:(hh + 1) * TOK, :] + mask[hh * TOK:(hh + 1) * TOK, :]))
    return out


def build_l1_attn(kb, xin, xout, gumbel_pc):
    hp = kb.hp
    I = hp.I
    G, T, S, V, SY = (kb.e[n] for n in ("gpsimd", "tensor", "scalar", "vector", "sync"))
    smm, sv, ss, sg, sld = (kb.sem(n) for n in ("smm", "sv", "ss", "sg", "sld"))
    swl = kb.sem("swl")
    H = 8
    W = np.asarray(I["L1_attn_in_proj_weight"], np.float32)
    bias = np.asarray(I["L1_attn_in_proj_bias"], np.float32)
    wqk_pc = []
    for c in range(NCORES):
        hh = (c % 2) * 8
        cols = slice(hh * 64, hh * 64 + 512)
        wq = np.ascontiguousarray(W[0:1024].T[:, cols])
        wk = np.ascontiguousarray(W[1024:2048].T[:, cols])
        wqk_pc.append(_block_w(np.concatenate([wq, wk], 1)))
    hp.add_percore("l1_wqk", wqk_pc)
    for m in range(8):
        vals = []
        for c in range(NCORES):
            hh = (c % 2) * 8
            j = m // 4
            sub = bias[j * 1024:(j + 1) * 1024][hh * 64: hh * 64 + 512]
            vals.append(sub[(m % 4) * 128:(m % 4) * 128 + 128])
        hp.add_percore_col(f"l1_b_{m}", vals)
    hp.add_col("c_tiny", np.full(128, 1e-30, np.float32))
    dm = np.zeros((128, 4 * 512), np.float32)
    for di, rel in enumerate((-384, -256, -128, 0)):
        p = np.arange(128)[:, None]
        j = np.arange(512)[None, :]
        dm[:, di * 512:(di + 1) * 512] = np.where(j < p - rel, 0.0, -1e9)
    hp.add_shared("l1_diagmask", dm)
    hp.add_percore("l1_gum", gumbel_pc)
    wqk = kb.dram_t("l1_wqk", [128, 1024 * 1024 // 128], F32, kind="ExternalInput")
    dm_d = kb.dram_t("l1_diagmask", [128, 2048], F32, kind="ExternalInput")
    gum_d = kb.dram_t("l1_gum", [TOK, TFULL], F32, kind="ExternalInput")
    hid_d = kb.dram_t("l1_hid", [1024, TOK], BF16)
    agh_d = kb.dram_t("l1_agh", [2048, TOK], BF16)
    part_d = kb.dram_t("l1_part", [TFULL, TFULL], BF16)
    rs_d = kb.dram_t("l1_rs", [TOK, TFULL], BF16)

    ar = kb.ar
    ar.push()
    ar.push()
    u = ar.alloc(BF16, DC * TOK)
    xs_buf = ar.alloc(F32, DC * TOK)
    xsv = emit_load_x(kb, xin, xs_buf)
    emit_ln(kb, xsv, lambda c: u[:, c * TOK:(c + 1) * TOK], TOK)
    G.wait(sv)
    for c in range(DC):
        G.dma_start(hid_d[c * 128:(c + 1) * 128, :], u[:, c * TOK:(c + 1) * TOK]).then_inc(kb.sem("sldg"), 16)
    kb.wait_dma(G)
    ar.pop()
    kb.collective("AllGather", ALU.bypass, [hid_d.ap().opt()], [agh_d.ap().opt()])

    def rhs_tile(kc, n0):
        if n0 < TOK:
            return agh_d[kc * 128:(kc + 1) * 128, n0:n0 + 512]
        return agh_d[1024 + kc * 128: 1024 + (kc + 1) * 128, n0 - TOK:n0 - TOK + 512]

    qbuf = ar.alloc(BF16, TFULL)
    kbuf = ar.alloc(BF16, TFULL)
    dmask = ar.alloc(F32, 2048)
    SY.dma_start(dmask, dm_d[:, :]).then_inc(sld, 16)
    mbuf = ar.alloc(F32, 32)
    mtmp = ar.alloc(F32, 2)
    mneg = ar.alloc(F32, 32)
    ssum = ar.alloc(F32, 32)
    stile = ar.alloc(F32, 32)
    b2 = ar.alloc(F32, 32)
    e_t = ar.alloc(BF16, 2 * 512)
    sc_f32 = ar.alloc(F32, 2 * 512)
    kb.wait_dma(V)

    def wslice(m0, mn):
        return wqk.ap()[:, m0 * 8 * 128:(m0 + mn) * 8 * 128]

    def epi_q(m, n0, nn, psv, pair):
        V.wait(smm)
        V.tensor_scalar(qbuf[:, n0:n0 + nn], psv, kb.col(f"l1_b_{pair}"), 0.125,
                        op0=ALU.add, op1=ALU.mult)

    def epi_k(m, n0, nn, psv, pair):
        V.wait(smm)
        V.tensor_scalar(kbuf[:, n0:n0 + nn], psv, kb.col(f"l1_b_{4 + pair}"), None, op0=ALU.add)

    for h in range(H):
        g = h // 2
        off = (h % 2) * 64
        if h % 2 == 0:
            T.wait(sv)
            emit_linear_dramrhs(kb, wslice(g, 1), 1024, 128, rhs_tile, TFULL,
                                lambda m, n0, nn, psv: epi_q(m, n0, nn, psv, g))
            emit_linear_dramrhs(kb, wslice(4 + g, 1), 1024, 128, rhs_tile, TFULL,
                                lambda m, n0, nn, psv: epi_k(m, n0, nn, psv, g))
            T.wait(sv)
        qv = lambda c0, nn: qbuf[off:off + 64, c0: c0 + nn]
        kv = lambda c0, nn: kbuf[off:off + 64, c0: c0 + nn]
        kb.refresh_banks()
        for qc in range(32):
            nk = (qc // 4 + 1) * 512
            nhalf = (nk + 2047) // 2048
            for hf in range(nhalf):
                k0 = hf * 2048
                kn = min(2048, nk - k0)
                T.wait(sv)
                T.wait(ss)
                for j in range((kn + 511) // 512):
                    kk = min(512, kn - j * 512)
                    inst = T.matmul(kb.ps_mm[:, j * 512:j * 512 + kk], qv(qc * 128, 128),
                                    kv(k0 + j * 512, kk), start=True, stop=True)
                inst.then_inc(smm, 1)
                V.wait(smm)
                dst = mbuf[:, qc:qc + 1] if nhalf == 1 else mtmp[:, hf:hf + 1]
                V.tensor_reduce(dst, kb.ps_mm[:, 0:kn], axis=mybir.AxisListType.X, op=ALU.max)
            if nhalf > 1:
                V.tensor_reduce(mbuf[:, qc:qc + 1], mtmp[:, 0:nhalf], axis=mybir.AxisListType.X, op=ALU.max)
        V.tensor_scalar(mneg[:, 0:32], mbuf[:, 0:32], -1.0, None, op0=ALU.mult)
        kb.refresh_banks()
        for qc in range(32):
            kts = qc // 4 + 1
            for kt in range(kts):
                q = kb.mm_q % 2
                kb.mm_q += 1
                psv = kb.ps_mm[:, q * 512: q * 512 + 512]
                kb.wait_mark(T, kb.bank_free[q])
                inst = T.matmul(psv, qv(qc * 128, 128), kv(kt * 512, 512), start=True, stop=True)
                inst.then_inc(smm, 1)
                rel = kt * 512 - qc * 128
                src_exp = psv
                if rel > -512:
                    di = (rel + 384) // 128
                    sc = sc_f32[:, (kt % 2) * 512:(kt % 2) * 512 + 512]
                    V.wait(smm)
                    V.wait(ss)
                    V.tensor_tensor(sc, psv, dmask[:, di * 512:(di + 1) * 512], op=ALU.add)
                    src_exp = sc
                S.wait(smm)
                S.wait(sv)
                S.activation(e_t[:, (kt % 2) * 512:(kt % 2) * 512 + 512], src_exp, AF.Exp,
                             bias=mneg[:, qc:qc + 1], accum_out=stile[:, kt:kt + 1])
                kb.bank_free[q] = kb.epi_mark()
            V.wait(ss)
            V.tensor_reduce(ssum[:, qc:qc + 1], stile[:, 0:kts], axis=mybir.AxisListType.X, op=ALU.add)
        S.wait(sv)
        S.activation(b2[:, 0:32], ssum[:, 0:32], AF.Ln, bias=kb.col("c_tiny"))
        V.wait(ss)
        V.tensor_tensor(b2[:, 0:32], mbuf[:, 0:32], b2[:, 0:32], op=ALU.add)
        V.tensor_scalar(b2[:, 0:32], b2[:, 0:32], -1.0, -math.log(16.0), op0=ALU.mult, op1=ALU.add)
        if h == 0:
            kb.dbg("l1_mbuf", mbuf[:, 0:32])
            kb.dbg("l1_ssum", ssum[:, 0:32])
            kb.dbg("l1_b2", b2[:, 0:32])
        S.wait(sv)
        kb.refresh_banks()
        for qc in range(32):
            kts = qc // 4 + 1
            for kt in range(kts):
                q = kb.mm_q % 2
                kb.mm_q += 1
                psv = kb.ps_mm[:, q * 512: q * 512 + 512]
                kb.wait_mark(T, kb.bank_free[q])
                inst = T.matmul(psv, qv(qc * 128, 128), kv(kt * 512, 512), start=True, stop=True)
                inst.then_inc(smm, 1)
                rel = kt * 512 - qc * 128
                src_exp = psv
                if rel > -512:
                    di = (rel + 384) // 128
                    sc = sc_f32[:, (kt % 2) * 512:(kt % 2) * 512 + 512]
                    V.wait(smm)
                    V.wait(ss)
                    V.wait(sg)
                    V.tensor_tensor(sc, psv, dmask[:, di * 512:(di + 1) * 512], op=ALU.add)
                    src_exp = sc
                    S.wait(sv)
                S.wait(smm)
                S.wait(sg)
                et = e_t[:, (kt % 2) * 512:(kt % 2) * 512 + 512]
                S.activation(et, src_exp, AF.Exp, bias=b2[:, qc:qc + 1])
                kb.bank_free[q] = kb.epi_mark()
                G.wait(ss)
                if h == 0:
                    G.dma_start(part_d[qc * 128:(qc + 1) * 128, kt * 512:(kt + 1) * 512],
                                et).then_inc(kb.sem("sldg"), 16)
                else:
                    G.dma_start(part_d[qc * 128:(qc + 1) * 128, kt * 512:(kt + 1) * 512],
                                et, accum_op=ALU.add).then_inc(kb.sem("sldg"), 16)
            if h == 0:
                for kt in range(kts, 8):
                    G.wait(sv)
                    G.dma_start(part_d[qc * 128:(qc + 1) * 128, kt * 512:(kt + 1) * 512],
                                kb.zt_bf).then_inc(kb.sem("sldg"), 16)
    kb.wait_dma(G)
    kb.collective("ReduceScatter", ALU.add, [part_d.ap().opt()], [rs_d.ap().opt()])

    htm = ar.alloc(BF16, 8 * TFULL)
    hslab = ar.alloc(BF16, TOK)
    for rk in range(2):
        for dc in range(DC):
            kb.wait_dma(SY)
            SY.wait(smm)
            SY.dma_start(hslab, agh_d[rk * 1024 + dc * 128: rk * 1024 + (dc + 1) * 128, :]).then_inc(sld, 16)
            kb.wait_dma(T)
            T.wait(ss)
            for tcl in range(16):
                tc = rk * 16 + tcl
                T.transpose(kb.ps_d.bitcast(BF16)[:, 0:128], hslab[:, tcl * 128:(tcl + 1) * 128],
                            kb.eye_bf[:, :]).then_inc(smm, 1)
                S.wait(smm)
                S.copy(htm[:, tc * 1024 + dc * 128: tc * 1024 + dc * 128 + 128],
                       kb.ps_d.bitcast(BF16)[:, 0:128])
                T.wait(ss)
    out_d = kb.dram_t("l1_out", [TOK, 1024], BF16)
    e2k = ar.alloc(BF16, 32 * 512)
    avt = ar.alloc(BF16, 2 * 512)
    gmt = ar.alloc(F32, 2 * 512)
    tmt = ar.alloc(F32, 2 * 512)
    stile2 = ar.alloc(F32, 32)
    rcol = ar.alloc(F32, 16)
    zeros1 = ar.alloc(F32, 4)
    m1 = ar.alloc(F32, 4)
    otile = ar.alloc(BF16, 2 * 512)
    V.memset(zeros1[:, 0:1], 0.0)
    V.memset(rcol[:, 0:16], 0.0)
    for qt in range(4):
        for qcl in range(4):
            qc = qt * 4 + qcl
            for kt in range(8):
                i = kt % 2
                kb.wait_dma(SY)
                SY.wait(sv)
                SY.dma_start(avt[:, i * 512:(i + 1) * 512],
                             rs_d[qc * 128:(qc + 1) * 128, kt * 512:(kt + 1) * 512]).then_inc(sld, 16)
                SY.dma_start(gmt[:, i * 512:(i + 1) * 512],
                             gum_d[qc * 128:(qc + 1) * 128, kt * 512:(kt + 1) * 512]).then_inc(sld, 16)
                kb.wait_dma(V)
                V.wait(ss)
                V.tensor_tensor(tmt[:, i * 512:(i + 1) * 512], gmt[:, i * 512:(i + 1) * 512],
                                avt[:, i * 512:(i + 1) * 512], op=ALU.add)
                S.wait(sv)
                S.wait(smm)
                S.activation(e2tm_slot(kb, e_t, kt), tmt[:, i * 512:(i + 1) * 512], AF.Exp,
                             scale=0.5, accum_out=stile2[:, kt:kt + 1])
                if qt == 0 and qcl == 0 and kt == 0:
                    kb.dbg("l1_tmt0", tmt[:, 0:512])
                    kb.dbg("l1_et0", e2tm_slot(kb, e_t, kt))
                T.wait(ss)
                for sub in range(4):
                    kc = kt * 4 + sub
                    T.transpose(kb.ps_d.bitcast(BF16)[:, 0:128],
                                e2tm_slot(kb, e_t, kt)[:, sub * 128:(sub + 1) * 128],
                                kb.eye_bf[:, :]).then_inc(smm, 1)
                    S.wait(smm)
                    S.copy(e2k[:, kc * 512 + qcl * 128: kc * 512 + qcl * 128 + 128],
                           kb.ps_d.bitcast(BF16)[:, 0:128])
                    T.wait(ss)
            V.wait(ss)
            V.tensor_reduce(m1[:, 0:1], stile2[:, 0:8], axis=mybir.AxisListType.X, op=ALU.add)
            if qt == 0 and qcl == 0:
                kb.dbg("l1_stile2", stile2[:, 0:8])
                kb.dbg("l1_s2", m1[:, 0:1])
            V.tensor_scalar(m1[:, 1:2], m1[:, 0:1], 0.0, None, op0=ALU.is_equal)
            V.tensor_tensor(m1[:, 2:3], m1[:, 0:1], m1[:, 1:2], op=ALU.add)
            V.reciprocal(m1[:, 2:3], m1[:, 2:3])
            V.tensor_scalar(m1[:, 1:2], m1[:, 1:2], -1.0, 1.0, op0=ALU.mult, op1=ALU.add)
            V.tensor_tensor(rcol[:, qcl * 4:qcl * 4 + 1], m1[:, 2:3], m1[:, 1:2], op=ALU.mult)
        if qt == 0:
            kb.dbg("l1_htm0", htm[:, 0:1024])
            kb.dbg("l1_e2k0", e2k[:, 0:2048])
            kb.dbg("l1_rcol", rcol[:, 0:16])
        kb.refresh_banks()
        for qcl in range(4):
            qc = qt * 4 + qcl
            for dn in range(2):
                q = kb.mm_q % 4
                kb.mm_q += 1
                psv = kb.ps_mm[:, q * 512: q * 512 + 512]
                kb.wait_mark(T, kb.bank_free[q])
                for kc in range(32):
                    inst = T.matmul(psv, e2k[:, kc * 512 + qcl * 128: kc * 512 + qcl * 128 + 128],
                                    htm[:, kc * 1024 + dn * 512: kc * 1024 + dn * 512 + 512],
                                    start=(kc == 0), stop=(kc == 31))
                inst.then_inc(smm, 1)
                V.wait(smm)
                kb.wait_dma(V)
                V.tensor_scalar(otile[:, (dn % 2) * 512:(dn % 2) * 512 + 512], psv,
                                rcol[:, qcl * 4:qcl * 4 + 1], None, op0=ALU.mult)
                kb.bank_free[q] = kb.epi_mark()
                SY.wait(sv)
                SY.dma_start(out_d[qc * 128:(qc + 1) * 128, dn * 512:(dn + 1) * 512],
                             otile[:, (dn % 2) * 512:(dn % 2) * 512 + 512]).then_inc(sld, 16)
    rst = ar.alloc(BF16, TOK)
    xio = ar.alloc(F32, 2 * 512)
    xio2 = ar.alloc(F32, 2 * 512)
    kb.wait_dma(S)
    for c in range(DC):
        S.wait(sv)
        S.dma_start(rst, out_d[:, c * 128:(c + 1) * 128], transpose=True).then_inc(kb.sem("slds"), 16)
        kb.wait_dma(V)
        for n in range(4):
            i = n % 2
            SY.wait(sv)
            SY.dma_start(xio[:, i * 512:(i + 1) * 512],
                         xin[c * 128:(c + 1) * 128, n * 512:(n + 1) * 512]).then_inc(sld, 16)
            kb.wait_dma(V)
            V.tensor_tensor(xio2[:, i * 512:(i + 1) * 512], xio[:, i * 512:(i + 1) * 512],
                            rst[:, n * 512:(n + 1) * 512], op=ALU.add)
            SY.wait(sv)
            SY.dma_start(xout[c * 128:(c + 1) * 128, n * 512:(n + 1) * 512],
                         xio2[:, i * 512:(i + 1) * 512]).then_inc(sld, 16)
    ar.pop()


def e2tm_slot(kb, e_t, kt):
    return e_t[:, (kt % 2) * 512:(kt % 2) * 512 + 512]




def emit_ln2(kb, xs, xsub_d, ubf, ntok, col0, g_col, b_col):
    G, T, S, V, SY = (kb.e[n] for n in ("gpsimd", "tensor", "scalar", "vector", "sync"))
    smm, sv, ss, sld = kb.sem("smm"), kb.sem("sv"), kb.sem("ss"), kb.sem("sld")
    ar = kb.ar
    ar.push()
    ntile = ntok // 512
    sq = ar.alloc(F32, 512)
    mu = ar.alloc(F32, ntok)[0:1, :]
    r2 = ar.alloc(F32, ntok)[0:1, :]
    brow = ar.alloc(F32, ntok)[0:1, :]
    a2row = ar.alloc(F32, ntok)[0:1, :]
    b2row = ar.alloc(F32, ntok)[0:1, :]
    rho = ar.alloc(F32, ntok)[0:1, :]
    tmp = ar.alloc(F32, 512)
    stage = ar.alloc(F32, 512)
    oc = kb.ones_col
    for n in range(ntile):
        nsl = slice(n * 512, (n + 1) * 512)
        for c in range(DC):
            S.wait(smm)
            S.activation(sq, xs(c)[:, nsl], AF.Square)
            if c == 0:
                T.wait(sv)
            T.matmul(kb.ps_a[0:1, 0:512], oc, xs(c)[:, nsl], start=(c == 0), stop=(c == DC - 1))
            T.wait(ss)
            inst = T.matmul(kb.ps_b[0:1, 0:512], oc, sq, start=(c == 0), stop=(c == DC - 1))
            inst.then_inc(smm, 1)
        V.wait(smm)
        V.tensor_scalar(mu[:, nsl], kb.ps_a[0:1, 0:512], 1.0 / D, None, op0=ALU.mult)
        V.tensor_scalar(r2[:, nsl], kb.ps_b[0:1, 0:512], 1.0 / D, None, op0=ALU.mult)
    V.tensor_tensor(brow, mu, mu, op=ALU.mult)
    V.tensor_tensor(r2, r2, brow, op=ALU.subtract)
    S.wait(sv)
    S.activation(rho, r2, AF.Ln, bias=kb.col("c_epsln")[0:1, :])
    S.activation(rho, rho, AF.Exp, scale=-1.0)
    V.wait(ss)
    V.tensor_tensor(a2row, r2, rho, op=ALU.mult)
    S.wait(sv)
    S.activation(r2, r2, AF.Ln, bias=kb.col("c_epsln")[0:1, :])
    S.activation(r2, r2, AF.Exp, scale=-0.5)
    S.activation(a2row, a2row, AF.Ln, bias=kb.col("c_epsln")[0:1, :])
    S.activation(a2row, a2row, AF.Exp, scale=-0.5)
    V.wait(ss)
    V.scalar_tensor_tensor(brow, mu, -1.0, r2, op0=ALU.mult, op1=ALU.mult)
    V.tensor_tensor(a2row, a2row, r2, op=ALU.mult)
    V.scalar_tensor_tensor(b2row, mu, -1.0, a2row, op0=ALU.mult, op1=ALU.mult)
    T.wait(sv)
    for n in range(ntile):
        nsl = slice(n * 512, (n + 1) * 512)
        T.matmul(kb.ps_c[:, 0:512], kb.ones_row, r2[:, nsl], start=True, stop=True)
        T.matmul(kb.ps_d[:, 0:512], kb.ones_row, brow[:, nsl], start=True, stop=True)
        T.matmul(kb.ps_a[:, 0:512], kb.ones_row, a2row[:, nsl], start=True, stop=True)
        inst = T.matmul(kb.ps_b[:, 0:512], kb.ones_row, b2row[:, nsl], start=True, stop=True)
        inst.then_inc(smm, 1)
        V.wait(smm)
        for c in range(DC):
            kb.wait_dma(V)
            V.tensor_tensor(stage, xs(c)[:, nsl], kb.ps_c[:, 0:512], op=ALU.mult)
            V.tensor_tensor(stage, stage, kb.ps_d[:, 0:512], op=ALU.add)
            SY.wait(sv)
            SY.dma_start(xsub_d[c * 128:(c + 1) * 128, col0 + n * 512: col0 + (n + 1) * 512],
                         stage).then_inc(sld, 16)
            V.tensor_tensor(tmp, xs(c)[:, nsl], kb.ps_a[:, 0:512], op=ALU.mult)
            V.tensor_tensor(tmp, tmp, kb.ps_b[:, 0:512], op=ALU.add)
            V.tensor_scalar(ubf(c)[:, nsl], tmp, g_col(c), b_col(c), op0=ALU.mult, op1=ALU.add)
        if n < ntile - 1:
            T.wait(sv)
    ar.pop()

def build_l0_attn(kb, xin, xout):
    hp = kb.hp
    I = hp.I
    G, T, S, V, SY = (kb.e[n] for n in ("gpsimd", "tensor", "scalar", "vector", "sync"))
    smm, sv, ss, sg, sld, swl = (kb.sem(n) for n in ("smm", "sv", "ss", "sg", "sld", "swl"))
    DI = 2048
    DIC = DI // 128
    NST = 16
    hp.add_shared("m_inproj", prep_wT(I["L0_attn_in_proj_w"]))
    xp = np.zeros((2048, 128), np.float32)
    wt = np.asarray(I["L0_attn_x_proj_w"], np.float32).T
    xp[:, 0:64] = wt[:, 0:64]
    xp[:, 64:80] = wt[:, 64:80]
    xp[:, 96:112] = wt[:, 80:96]
    hp.add_shared("m_xproj", _block_w(xp))
    hp.add_shared("m_dtw", prep_wT(I["L0_attn_dt_w"]))
    hp.add_shared("m_outw", prep_wT(I["L0_attn_out_w"]))
    hp.add_shared("m_ffg", prep_wT(I["L0_attn_ffg_w"]))
    hp.add_shared("m_ffp", prep_wT(I["L0_attn_ffp_w"]))
    hp.add_cols("m_n1g", I["L0_attn_n1_g"])
    hp.add_cols("m_n1b", I["L0_attn_n1_b"])
    hp.add_cols("m_n2g", I["L0_attn_n2_g"])
    hp.add_cols("m_n2b", I["L0_attn_n2_b"])
    hp.add_cols("m_convb", I["L0_attn_conv_b"])
    hp.add_cols("m_dtb", I["L0_attn_dt_b"])
    hp.add_cols("m_D", I["L0_attn_D"])
    hp.add_cols("m_ffgb", I["L0_attn_ffg_b"])
    hp.add_cols("m_ffpb", I["L0_attn_ffp_b"])
    cw = np.asarray(I["L0_attn_conv_w"], np.float32)[:, 0, :]
    for i in range(4):
        hp.add_cols(f"m_cw{i}", cw[:, i])
    A = -np.exp(np.asarray(I["L0_attn_A_log"], np.float32))
    for n in range(NST):
        hp.add_cols(f"m_A{n}", A[:, n])
    hp.add_percore_col("m_flag", [np.full(128, float(c % 2), np.float32) for c in range(NCORES)])
    ipw = kb.dram_t("m_inproj", [128, 1024 * 4096 // 128], F32, kind="ExternalInput")
    xpj = kb.dram_t("m_xproj", [128, 2048 * 128 // 128], F32, kind="ExternalInput")
    dtw = kb.dram_t("m_dtw", [128, 128 * 2048 // 128], F32, kind="ExternalInput")
    outw = kb.dram_t("m_outw", [128, 2048 * 1024 // 128], F32, kind="ExternalInput")
    ffgw = kb.dram_t("m_ffg", [128, 1024 * 2048 // 128], F32, kind="ExternalInput")
    ffpw = kb.dram_t("m_ffp", [128, 1024 * 1024 // 128], F32, kind="ExternalInput")
    xsub_d = kb.dram_t("m_xsub", [1024, TOK], F32)
    z_d = kb.dram_t("m_z", [DI, TOK], BF16)
    xc_d = kb.dram_t("m_xc", [DI, TOK], BF16)
    halo_d = kb.dram_t("m_halo", [DI, 4], F32)
    haloag_d = kb.dram_t("m_haloag", [2 * DI, 4], F32)
    du_d = kb.dram_t("m_du", [2 * DI, TOK], F32)
    duag_d = kb.dram_t("m_duag", [4 * DI, TOK], F32)
    bc_d = kb.dram_t("m_bc", [64, TOK], F32)
    bcag_d = kb.dram_t("m_bcag", [128, TOK], F32)
    y_d = kb.dram_t("m_y", [DI, TOK], F32)

    ar = kb.ar
    ar.push()
    ar.push()
    u = ar.alloc(BF16, DC * TOK)
    ar.push()
    for th in range(2):
        ar.push()
        xs_buf = ar.alloc(F32, DC * 1024)
        xsv = emit_load_x(kb, xin, xs_buf, ntok=1024, col0=th * 1024)
        emit_ln2(kb, xsv, xsub_d,
                 (lambda th2: (lambda c: u[:, c * TOK + th2 * 1024: c * TOK + th2 * 1024 + 1024]))(th),
                 1024, th * 1024, lambda c: kb.col(f"m_n1g_{c}"), lambda c: kb.col(f"m_n1b_{c}"))
        ar.pop()
    ar.pop()
    T.wait(sv)
    xi_t = ar.alloc(BF16, DIC * (TOK + 4))
    uview = lambda kc, n0, nn: u[:, kc * TOK + n0: kc * TOK + n0 + nn]

    def epi_ip(m, n0, nn, psv):
        if m < DIC:
            S.wait(smm)
            S.copy(xi_t[:, m * (TOK + 4) + 4 + n0: m * (TOK + 4) + 4 + n0 + nn], psv)
        else:
            mm = m - DIC
            S.wait(smm)
            kb.wait_dma(S)
            S.copy(zstage[:, 0:nn], psv)
            SY.wait(ss)
            SY.dma_start(z_d[mm * 128:(mm + 1) * 128, n0:n0 + nn], zstage[:, 0:nn]).then_inc(sld, 16)
    zstage = ar.alloc(BF16, 512)
    emit_linear(kb, ipw.ap(), 1024, 4096, uview, TOK, epi_ip)
    halo_st = ar.alloc(F32, 64)
    V.memset(halo_st[:, :], 0.0)
    T.wait(ss)
    for ch in range(DIC):
        V.wait(ss)
        V.tensor_copy(halo_st[:, ch * 4: ch * 4 + 3], xi_t[:, ch * (TOK + 4) + 4 + TOK - 3: ch * (TOK + 4) + 4 + TOK])
    G.wait(sv)
    G.dma_start(halo_d[:, :].rearrange("(ch p) c -> p ch c", p=128, c=4),
                halo_st[:, 0:64].rearrange("p (ch c) -> p ch c", c=4)).then_inc(kb.sem("sldg"), 16)
    kb.wait_dma(G)
    kb.collective("AllGather", ALU.bypass, [halo_d.ap().opt()], [haloag_d.ap().opt()])
    hl = ar.alloc(F32, 64)
    SY.dma_start(hl[:, 0:64].rearrange("p (ch c) -> p ch c", c=4),
                 haloag_d[0:DI, :].rearrange("(ch p) c -> p ch c", p=128, c=4)).then_inc(sld, 16)
    kb.wait_dma(V)
    for ch in range(DIC):
        V.tensor_scalar(xi_t[:, ch * (TOK + 4) + 1: ch * (TOK + 4) + 4], hl[:, ch * 4: ch * 4 + 3],
                        kb.col("m_flag"), None, op0=ALU.mult)
    acc = ar.alloc(F32, TOK)
    xcstg = ar.alloc(BF16, TOK)
    for ch in range(DIC):
        base = ch * (TOK + 4) + 4
        V.wait(ss)
        V.tensor_scalar(acc, xi_t[:, base - 3: base - 3 + TOK], kb.col(f"m_cw0_{ch}"), None, op0=ALU.mult)
        for i in range(1, 4):
            V.scalar_tensor_tensor(acc, xi_t[:, base - 3 + i: base - 3 + i + TOK],
                                   kb.col(f"m_cw{i}_{ch}"), acc, op0=ALU.mult, op1=ALU.add)
        S.wait(sv)
        kb.wait_dma(S)
        S.activation(xcstg, acc, AF.Silu, bias=kb.col(f"m_convb_{ch}"))
        SY.wait(ss)
        SY.dma_start(xc_d[ch * 128:(ch + 1) * 128, :], xcstg).then_inc(sld, 16)
    for e in (T, S, V, SY, G):
        e.wait(sv)
        e.wait(ss)
        kb.wait_dma(e)
    ar.pop()
    dbc = ar.alloc(BF16, TOK)
    V.memset(dbc[:, :], 0.0)

    def epi_xp(m, n0, nn, psv):
        S.wait(smm)
        S.copy(dbc[0:64, n0:n0 + nn], psv[0:64, 0:nn])
        V.wait(smm)
        kb.wait_dma(V)
        V.tensor_copy(bct[0:32, 0:nn], psv[64:96, 0:nn])
        V.tensor_copy(bct2[0:32, 0:nn], psv[96:128, 0:nn])
        SY.wait(sv)
        SY.dma_start(bc_d[0:32, n0:n0 + nn], bct[0:32, 0:nn]).then_inc(sld, 16)
        SY.dma_start(bc_d[32:64, n0:n0 + nn], bct2[0:32, 0:nn]).then_inc(sld, 16)
    bct = ar.alloc(F32, 512)
    bct2 = ar.alloc(F32, 512)
    emit_linear_dramrhs(kb, xpj.ap(), 2048, 128,
                        lambda kc, n0: xc_d[kc * 128:(kc + 1) * 128, n0:n0 + 512], TOK, epi_xp)
    def epi_dt(m, n0, nn, psv):
        S.wait(smm)
        S.wait(sv)
        S.activation(sp1[:, 0:nn], psv, AF.Abs, bias=kb.col(f"m_dtb_{m}"))
        S.activation(sp1[:, 0:nn], sp1[:, 0:nn], AF.Exp, scale=-1.0)
        S.activation(sp1[:, 0:nn], sp1[:, 0:nn], AF.Ln, bias=kb.col("c_one"))
        S.activation(sp2[:, 0:nn], psv, AF.Relu, bias=kb.col(f"m_dtb_{m}"))
        SY.wait(sv)
        SY.dma_start(xcep[:, 0:nn], xc_d[m * 128:(m + 1) * 128, n0:n0 + nn]).then_inc(sld, 16)
        V.wait(ss)
        kb.wait_dma(V)
        V.tensor_tensor(dstage[:, 0:nn], sp1[:, 0:nn], sp2[:, 0:nn], op=ALU.add)
        V.tensor_tensor(ustage[:, 0:nn], dstage[:, 0:nn], xcep[:, 0:nn], op=ALU.mult)
        SY.wait(sv)
        SY.dma_start(du_d[m * 128:(m + 1) * 128, n0:n0 + nn], dstage[:, 0:nn]).then_inc(sld, 16)
        SY.dma_start(du_d[DI + m * 128: DI + (m + 1) * 128, n0:n0 + nn], ustage[:, 0:nn]).then_inc(sld, 16)
    sp1 = ar.alloc(F32, 512)
    sp2 = ar.alloc(F32, 512)
    dstage = ar.alloc(F32, 512)
    ustage = ar.alloc(F32, 512)
    xcep = ar.alloc(BF16, 512)
    emit_linear(kb, dtw.ap(), 128, 2048, lambda kc, n0, nn: dbc[0:128, n0:n0 + nn], TOK, epi_dt)
    kb.wait_dma(G)
    kb.collective("AllGather", ALU.bypass, [du_d.ap().opt()], [duag_d.ap().opt()])
    kb.collective("AllGather", ALU.bypass, [bc_d.ap().opt()], [bcag_d.ap().opt()])
    ar.pop()
    ar.push()
    onesrow_bf = ar.alloc(BF16, 128)[0:1, :]
    V.memset(onesrow_bf, 1.0)
    brow_t = ar.alloc(BF16, TFULL)[0:1, :]
    crow_t = ar.alloc(BF16, TFULL)[0:1, :]
    bbc = ar.alloc(BF16, TFULL)
    cbc = ar.alloc(BF16, TFULL)
    dsl = ar.alloc(F32, TFULL)
    usl = ar.alloc(F32, TFULL)
    a_t = ar.alloc(F32, TFULL)
    b_t = ar.alloc(F32, TFULL)
    h_t = ar.alloc(F32, TFULL)
    yac = ar.alloc(F32, TFULL)
    ytmp = b_t
    kb.wait_dma(T)
    for ch in range(DIC):
        kb.wait_dma(SY)
        SY.wait(sv)
        SY.wait(ss)
        SY.dma_start(dsl[:, 0:TOK], duag_d[ch * 128:(ch + 1) * 128, :]).then_inc(sld, 16)
        SY.dma_start(dsl[:, TOK:TFULL], duag_d[2 * DI + ch * 128: 2 * DI + (ch + 1) * 128, :]).then_inc(sld, 16)
        SY.dma_start(usl[:, 0:TOK], duag_d[DI + ch * 128: DI + (ch + 1) * 128, :]).then_inc(sld, 16)
        SY.dma_start(usl[:, TOK:TFULL], duag_d[3 * DI + ch * 128: 3 * DI + (ch + 1) * 128, :]).then_inc(sld, 16)
        kb.wait_dma(V)
        kb.wait_dma(S)
        for n in range(NST):
            G.wait(smm)
            G.dma_start(brow_t[:, 0:TOK], bcag_d[n:n + 1, :]).then_inc(kb.sem("sldg"), 16)
            G.dma_start(brow_t[:, TOK:TFULL], bcag_d[64 + n:65 + n, :]).then_inc(kb.sem("sldg"), 16)
            G.dma_start(crow_t[:, 0:TOK], bcag_d[32 + n:33 + n, :]).then_inc(kb.sem("sldg"), 16)
            G.dma_start(crow_t[:, TOK:TFULL], bcag_d[96 + n:97 + n, :]).then_inc(kb.sem("sldg"), 16)
            T.wait(sv)
            T.wait(ss)
            kb.wait_dma(T)
            for j in range(8):
                T.matmul(kb.ps_c[:, 0:512], onesrow_bf, brow_t[:, j * 512:(j + 1) * 512],
                         start=True, stop=True)
                inst = T.matmul(kb.ps_d[:, 0:512], onesrow_bf, crow_t[:, j * 512:(j + 1) * 512],
                                start=True, stop=True)
                inst.then_inc(smm, 1)
                S.wait(smm)
                S.copy(bbc[:, j * 512:(j + 1) * 512], kb.ps_c[:, 0:512])
                S.copy(cbc[:, j * 512:(j + 1) * 512], kb.ps_d[:, 0:512])
                T.wait(ss)
            S.wait(sv)
            S.activation(a_t, dsl, AF.Exp, scale=kb.col(f"m_A{n}_{ch}"))
            V.wait(ss)
            V.tensor_tensor(b_t, usl, bbc, op=ALU.mult)
            V.tensor_tensor_scan(h_t, a_t, b_t, 0.0, op0=ALU.mult, op1=ALU.add)
            V.tensor_tensor(ytmp, h_t, cbc, op=ALU.mult)
            if n == 0:
                V.tensor_copy(yac, ytmp)
            else:
                V.tensor_tensor(yac, yac, ytmp, op=ALU.add)
        V.tensor_tensor(ytmp[:, 0:TOK], yac[:, TOK:TFULL], yac[:, 0:TOK], op=ALU.subtract)
        V.scalar_tensor_tensor(ytmp[:, 0:TOK], ytmp[:, 0:TOK], kb.col("m_flag"), yac[:, 0:TOK],
                               op0=ALU.mult, op1=ALU.add)
        SY.wait(sv)
        SY.dma_start(y_d[ch * 128:(ch + 1) * 128, :], ytmp[:, 0:TOK]).then_inc(sld, 16)
    ar.pop()
    ar.push()
    ysl = ar.alloc(F32, TOK)
    zsl = ar.alloc(BF16, TOK)
    xcsl = ar.alloc(BF16, TOK)
    szl = ar.alloc(F32, TOK)
    ybf = ar.alloc(BF16, DIC * TOK)
    for ch in range(DIC):
        kb.wait_dma(SY)
        SY.wait(sv)
        SY.wait(ss)
        SY.dma_start(ysl, y_d[ch * 128:(ch + 1) * 128, :]).then_inc(sld, 16)
        SY.dma_start(zsl, z_d[ch * 128:(ch + 1) * 128, :]).then_inc(sld, 16)
        SY.dma_start(xcsl, xc_d[ch * 128:(ch + 1) * 128, :]).then_inc(sld, 16)
        kb.wait_dma(S)
        S.activation(szl, zsl, AF.Silu)
        kb.wait_dma(V)
        V.wait(ss)
        V.scalar_tensor_tensor(ysl, xcsl, kb.col(f"m_D_{ch}"), ysl, op0=ALU.mult, op1=ALU.add)
        V.tensor_tensor(ybf[:, ch * TOK:(ch + 1) * TOK], ysl, szl, op=ALU.mult)
    T.wait(sv)
    res1 = make_residual_epi(kb, xsub_d, xsub_d, alpha=1.0)
    emit_linear(kb, outw.ap(), 2048, 1024, lambda kc, n0, nn: ybf[:, kc * TOK + n0: kc * TOK + n0 + nn],
                TOK, res1)
    ar.pop()
    ar.push()
    u2 = ar.alloc(BF16, DC * TOK)
    ar.push()
    xs2 = ar.alloc(F32, DC * TOK)
    xs2v = emit_load_x(kb, xsub_d, xs2)
    emit_ln(kb, xs2v, lambda c: u2[:, c * TOK:(c + 1) * TOK], TOK,
            g_col=lambda c: kb.col(f"m_n2g_{c}"), b_col=lambda c: kb.col(f"m_n2b_{c}"))
    ar.pop()
    gv = ar.alloc(BF16, 16 * TOK)
    T.wait(sv)
    u2v = lambda kc, n0, nn: u2[:, kc * TOK + n0: kc * TOK + n0 + nn]
    def epi_gv(m, n0, nn, psv):
        V.wait(smm)
        V.tensor_scalar(gv[:, m * TOK + n0: m * TOK + n0 + nn], psv, kb.col(f"m_ffgb_{m}"),
                        None, op0=ALU.add)
    emit_linear(kb, ffgw.ap(), 1024, 2048, u2v, TOK, epi_gv)
    hgl = ar.alloc(BF16, DC * TOK)
    sg1 = ar.alloc(F32, 512)
    for ch in range(DC):
        for n in range(4):
            nsl = slice(n * 512, (n + 1) * 512)
            S.wait(sv)
            S.activation(sg1, gv[:, ch * TOK:(ch + 1) * TOK][:, nsl], AF.Silu)
            V.wait(ss)
            V.tensor_tensor(hgl[:, ch * TOK:(ch + 1) * TOK][:, nsl], sg1,
                            gv[:, (DC + ch) * TOK:(DC + ch + 1) * TOK][:, nsl], op=ALU.mult)
    T.wait(sv)
    res2 = make_residual_epi(kb, xsub_d, xsub_d, alpha=1.0)
    emit_linear(kb, ffpw.ap(), 1024, 1024, lambda kc, n0, nn: hgl[:, kc * TOK + n0: kc * TOK + n0 + nn],
                TOK, res2)
    xbt = ar.alloc(F32, 512)
    for ch in range(DC):
        for n in range(4):
            kb.wait_dma(SY)
            SY.wait(sv)
            SY.dma_start(xbt, xsub_d[ch * 128:(ch + 1) * 128, n * 512:(n + 1) * 512]).then_inc(sld, 16)
            kb.wait_dma(V)
            V.tensor_scalar(xbt, xbt, kb.col(f"m_ffpb_{ch}"), None, op0=ALU.add)
            SY.wait(sv)
            SY.dma_start(xsub_d[ch * 128:(ch + 1) * 128, n * 512:(n + 1) * 512], xbt).then_inc(sld, 16)
    xf1 = ar.alloc(F32, 512)
    xf2 = ar.alloc(F32, 512)
    for ch in range(DC):
        for n in range(4):
            kb.wait_dma(SY)
            SY.wait(sv)
            SY.dma_start(xf1, xin[ch * 128:(ch + 1) * 128, n * 512:(n + 1) * 512]).then_inc(sld, 16)
            SY.dma_start(xf2, xsub_d[ch * 128:(ch + 1) * 128, n * 512:(n + 1) * 512]).then_inc(sld, 16)
            kb.wait_dma(V)
            V.tensor_tensor(xf1, xf1, xf2, op=ALU.add)
            SY.wait(sv)
            SY.dma_start(xout[ch * 128:(ch + 1) * 128, n * 512:(n + 1) * 512], xf1).then_inc(sld, 16)
    ar.pop()



def build_l2_attn(kb, xin, xout):
    hp = kb.hp
    I = hp.I
    G, T, S, V, SY = (kb.e[n] for n in ("gpsimd", "tensor", "scalar", "vector", "sync"))
    smm, sep, sv, ss, sld = (kb.sem(n) for n in ("smm", "sep", "sv", "ss", "sld"))
    HD, C = 128, 64
    J = 2 * C
    NCH = TFULL // C
    Wq = np.asarray(I["L2_attn_Wq"], np.float32)
    Wk = np.asarray(I["L2_attn_Wk"], np.float32)
    Wv = np.asarray(I["L2_attn_Wv"], np.float32)
    Wf2 = np.asarray(I["L2_attn_Wf2"], np.float32)
    Wog2 = np.asarray(I["L2_attn_Wog2"], np.float32)
    Wo = np.asarray(I["L2_attn_Wo"], np.float32)
    nw = np.asarray(I["L2_attn_norm_w"], np.float32)
    Wbeta = np.asarray(I["L2_attn_Wbeta"], np.float32)
    qkv_pc, f2_pc, og2_pc, wo_pc, bwp_pc = [], [], [], [], []
    for c in range(NCORES):
        hh = (c % 2) * 4
        cols = slice(hh * HD, hh * HD + 512)
        qkv_pc.append(_block_w(np.concatenate([Wq[:, cols], Wk[:, cols], Wv[:, cols]], 1)))
        f2_pc.append(_block_w(np.ascontiguousarray(Wf2[:, cols])))
        og2_pc.append(_block_w(np.ascontiguousarray(Wog2[:, cols])))
        wrow = Wo[hh * HD: hh * HD + 512, :] * nw[hh * HD: hh * HD + 512][:, None]
        wo_pc.append(np.ascontiguousarray(wrow.reshape(4, 128, 1024).transpose(1, 0, 2).reshape(128, 4096)))
        bw = np.zeros((1024, 128), np.float32)
        bw[:, 0:4] = Wbeta[:, hh // HD * 0 + (c % 2) * 4: (c % 2) * 4 + 4]
        bwp_pc.append(_block_w(bw))
    hp.add_percore("d2_qkv", qkv_pc)
    hp.add_percore("d2_wf2", f2_pc)
    hp.add_percore("d2_wog2", og2_pc)
    hp.add_percore("d2_wo", wo_pc)
    hp.add_percore("d2_wbp", bwp_pc)
    hp.add_shared("d2_wf1", prep_w(np.asarray(I["L2_attn_Wf1"], np.float32)))
    hp.add_shared("d2_wog1", prep_w(np.asarray(I["L2_attn_Wog1"], np.float32)))
    t_ = np.arange(C)
    lt_lo = (t_[:, None] > t_[None, :]).astype(np.float32)
    lt_ge = (t_[:, None] >= t_[None, :]).astype(np.float32)
    lm = np.zeros((J, J), np.float32)
    lm[0:C, 0:C] = lt_lo; lm[0:C, C:J] = lt_lo
    lm[C:J, 0:C] = lt_ge; lm[C:J, C:J] = lt_lo
    hp.add_shared("d2_lm", lm)
    hp.add_shared("d2_lmT", np.ascontiguousarray(lm.T))
    kmT = np.concatenate([lt_lo.T, lt_lo.T], axis=1)
    hp.add_shared("d2_kmT", np.ascontiguousarray(kmT))
    qmT = np.concatenate([lt_ge.T.T * 0 + (t_[:, None] <= t_[None, :]).astype(np.float32)] * 2, axis=0)
    hp.add_shared("d2_qmT", np.ascontiguousarray(qmT))
    hp.add_shared("d2_qkmT", (t_[:, None] <= t_[None, :]).astype(np.float32))
    qkvw = kb.dram_t("d2_qkv", [128, 12288], F32, kind="ExternalInput")
    wf1 = kb.dram_t("d2_wf1", [128, 1024], F32, kind="ExternalInput")
    wog1 = kb.dram_t("d2_wog1", [128, 1024], F32, kind="ExternalInput")
    wf2 = kb.dram_t("d2_wf2", [128, 512], F32, kind="ExternalInput")
    wog2 = kb.dram_t("d2_wog2", [128, 512], F32, kind="ExternalInput")
    wo_d = kb.dram_t("d2_wo", [128, 4096], F32, kind="ExternalInput")
    wbp = kb.dram_t("d2_wbp", [128, 1024], F32, kind="ExternalInput")
    lm_d = kb.dram_t("d2_lm", [128, 128], F32, kind="ExternalInput")
    lmT_d = kb.dram_t("d2_lmT", [128, 128], F32, kind="ExternalInput")
    kmT_d = kb.dram_t("d2_kmT", [64, 128], F32, kind="ExternalInput")
    qmT_d = kb.dram_t("d2_qmT", [128, 64], F32, kind="ExternalInput")
    qkmT_d = kb.dram_t("d2_qkmT", [64, 64], F32, kind="ExternalInput")
    hid_d = kb.dram_t("d2_hid", [1024, TOK], BF16)
    agh_d = kb.dram_t("d2_agh", [2048, TOK], BF16)
    q_d = kb.dram_t("d2_q", [512, TFULL], BF16)
    k_d = kb.dram_t("d2_k", [512, TFULL], F32)
    v_d = kb.dram_t("d2_v", [512, TFULL], BF16)
    logf_d = kb.dram_t("d2_logf", [512, TFULL], F32)
    beta_d = kb.dram_t("d2_beta", [4, TFULL], F32)
    g_d = kb.dram_t("d2_g", [512, TFULL], BF16)
    o_d = kb.dram_t("d2_o", [512, TFULL], BF16)
    part_d = kb.dram_t("d2_part", [TFULL, 1024], BF16)
    rs_d = kb.dram_t("d2_rs", [TOK, 1024], BF16)

    ar = kb.ar
    ar.push()
    ar.push()
    u = ar.alloc(BF16, DC * TOK)
    xs_buf = ar.alloc(F32, DC * TOK)
    xsv = emit_load_x(kb, xin, xs_buf)
    emit_ln(kb, xsv, lambda c: u[:, c * TOK:(c + 1) * TOK], TOK)
    G.wait(sv)
    for c in range(DC):
        G.dma_start(hid_d[c * 128:(c + 1) * 128, :], u[:, c * TOK:(c + 1) * TOK]).then_inc(kb.sem("sldg"), 16)
    kb.wait_dma(G)
    ar.pop()
    kb.collective("AllGather", ALU.bypass, [hid_d.ap().opt()], [agh_d.ap().opt()])

    def rhs_tile(kc, n0):
        if n0 < TOK:
            return agh_d[kc * 128:(kc + 1) * 128, n0:n0 + 512]
        return agh_d[1024 + kc * 128: 1024 + (kc + 1) * 128, n0 - TOK:n0 - TOK + 512]

    stq = ar.alloc(BF16, 512)
    stk = ar.alloc(F32, 512)

    def epi_qkv(m, n0, nn, psv):
        S.wait(smm)
        kb.wait_dma(S)
        if 4 <= m < 8:
            S.activation(stk[:, 0:nn], psv, AF.Silu)
            SY.wait(ss)
            SY.dma_start(k_d[(m - 4) * 128:(m - 3) * 128, n0:n0 + nn], stk[:, 0:nn]).then_inc(sld, 16)
        else:
            S.activation(stq[:, 0:nn], psv, AF.Silu)
            SY.wait(ss)
            dst, mm_ = (q_d, m) if m < 4 else (v_d, m - 8)
            SY.dma_start(dst[mm_ * 128:(mm_ + 1) * 128, n0:n0 + nn], stq[:, 0:nn]).then_inc(sld, 16)
    emit_linear_dramrhs(kb, qkvw.ap(), 1024, 1536, rhs_tile, TFULL, epi_qkv)

    f1b = ar.alloc(BF16, TFULL)
    og1b = ar.alloc(BF16, TFULL)
    emit_linear_dramrhs(kb, wf1.ap(), 1024, 128, rhs_tile, TFULL,
                        epi_act(kb, AF.Copy, lambda m, n0, nn: f1b[:, n0:n0 + nn]))
    emit_linear_dramrhs(kb, wog1.ap(), 1024, 128, rhs_tile, TFULL,
                        epi_act(kb, AF.Copy, lambda m, n0, nn: og1b[:, n0:n0 + nn]))

    sp1 = ar.alloc(F32, 512)
    sp2 = ar.alloc(F32, 512)
    lfst = ar.alloc(F32, 512)

    def epi_lf(m, n0, nn, psv):
        S.wait(smm)
        kb.wait_dma(S)
        S.activation(sp1[:, 0:nn], psv, AF.Abs)
        S.activation(sp1[:, 0:nn], sp1[:, 0:nn], AF.Exp, scale=-1.0)
        S.activation(sp1[:, 0:nn], sp1[:, 0:nn], AF.Ln, bias=kb.col("c_one"))
        S.activation(sp2[:, 0:nn], psv, AF.Relu, scale=-1.0)
        V.wait(ss)
        kb.wait_dma(V)
        V.scalar_tensor_tensor(lfst[:, 0:nn], sp1[:, 0:nn], -1.0, sp2[:, 0:nn],
                               op0=ALU.mult, op1=ALU.subtract)
        SY.wait(sv)
        SY.dma_start(logf_d[m * 128:(m + 1) * 128, n0:n0 + nn], lfst[:, 0:nn]).then_inc(sld, 16)
    T.wait(ss)
    emit_linear(kb, wf2.ap(), 128, 512, lambda kc, n0, nn: f1b[:, n0:n0 + nn], TFULL, epi_lf)

    stg = ar.alloc(BF16, 512)

    def epi_g(m, n0, nn, psv):
        S.wait(smm)
        kb.wait_dma(S)
        S.activation(stg[:, 0:nn], psv, AF.Sigmoid)
        SY.wait(ss)
        SY.dma_start(g_d[m * 128:(m + 1) * 128, n0:n0 + nn], stg[:, 0:nn]).then_inc(sld, 16)
    T.wait(ss)
    emit_linear(kb, wog2.ap(), 128, 512, lambda kc, n0, nn: og1b[:, n0:n0 + nn], TFULL, epi_g)

    btst = ar.alloc(F32, 512)

    def epi_beta(m, n0, nn, psv):
        S.wait(smm)
        S.wait(sv)
        kb.wait_dma(S)
        S.activation(btst[:, 0:nn], psv, AF.Sigmoid)
        V.wait(ss)
        kb.wait_dma(V)
        V.tensor_scalar(btst[:, 0:nn], btst[:, 0:nn], 2.0, None, op0=ALU.mult)
        SY.wait(sv)
        SY.dma_start(beta_d[0:4, n0:n0 + nn], btst[0:4, 0:nn]).then_inc(sld, 16)
    emit_linear_dramrhs(kb, wbp.ap(), 1024, 128, rhs_tile, TFULL, epi_beta)

    kb.refresh_banks()
    lmask = ar.alloc(F32, 128)
    lmaskT = ar.alloc(F32, 128)
    kmT = ar.alloc(F32, 128)
    qmT = ar.alloc(F32, 64)
    qkmT = ar.alloc(F32, 64)
    onesf = ar.alloc(F32, 64)
    SY.dma_start(lmask, lm_d[:, :]).then_inc(sld, 16)
    SY.dma_start(lmaskT, lmT_d[:, :]).then_inc(sld, 16)
    SY.dma_start(kmT[0:64, :], kmT_d[:, :]).then_inc(sld, 16)
    SY.dma_start(qmT, qmT_d[:, :]).then_inc(sld, 16)
    SY.dma_start(qkmT[0:64, 0:64], qkmT_d[:, :]).then_inc(sld, 16)
    V.memset(onesf, 1.0)
    vtm_sb = ar.alloc(BF16, 32 * 512)
    kb.wait_dma(S)
    S.wait(sv)
    for tc in range(32):
        S.dma_start(vtm_sb[:, tc * 512:(tc + 1) * 512],
                    v_d[:, tc * 128:(tc + 1) * 128], transpose=True).then_inc(kb.sem("slds"), 16)
    qbf = ar.alloc(BF16, 64)
    qch = ar.alloc(F32, 64)
    kchunk = ar.alloc(F32, 64)
    gch = ar.alloc(F32, 64)
    vv_f = ar.alloc(F32, 128)
    brow = ar.alloc(F32, 64)[0:1, :]
    cum = ar.alloc(F32, 64)
    P = ar.alloc(F32, 64)
    Pm = ar.alloc(F32, 64)
    Ppm = ar.alloc(F32, 64)
    Pprev = ar.alloc(F32, 64)
    knorm = ar.alloc(F32, 64)
    rrow = ar.alloc(F32, 64)[0:1, :]
    t1 = ar.alloc(F32, 64)
    U_ = ar.alloc(F32, 128)
    Ct_ = ar.alloc(F32, 128)
    khat = ar.alloc(F32, 64)
    qt = ar.alloc(F32, 64)
    L_sb = ar.alloc(F32, 128)
    LT_sb = ar.alloc(F32, 128)
    LkT_sb = ar.alloc(F32, 128)
    A1T_sb = ar.alloc(F32, 64)
    A2T_sb = ar.alloc(F32, 64)
    X_sb = ar.alloc(F32, 128)
    Usm_sb = ar.alloc(F32, 128)
    ksm_sb = ar.alloc(F32, 128)
    S_sb = ar.alloc(F32, 128)
    Sscr = ar.alloc(F32, 128)
    o_tm = ar.alloc(BF16, 128)
    o_fm = ar.alloc(BF16, 64)
    psA = kb.ps_mm[:, 0:128]
    psB = kb.ps_mm[:, 512:640]
    psX = kb.ps_mm[:, 1024:1152]
    psW = kb.ps_mm[:, 1536:1664]
    ps_o = kb.ps_a[0:64, 0:128]
    eye64 = kb.eye_bf_t[0:64, 0:64]
    for h in range(4):
        V.memset(S_sb, 0.0)
        for ch in range(NCH):
            t0 = ch * C
            SY.wait(sv); SY.wait(ss); SY.wait(smm)
            SY.dma_start(qbf, q_d[h * 128:(h + 1) * 128, t0:t0 + C]).then_inc(sld, 16)
            SY.dma_start(kchunk, k_d[h * 128:(h + 1) * 128, t0:t0 + C]).then_inc(sld, 16)
            SY.dma_start(gch, logf_d[h * 128:(h + 1) * 128, t0:t0 + C]).then_inc(sld, 16)
            SY.dma_start(brow, beta_d[h:h + 1, t0:t0 + C]).then_inc(sld, 16)
            S.wait(smm)
            S.wait(sv)
            kb.wait_dma(S)
            S.copy(qch, qbf)
            r0 = t0 % 128
            S.copy(vv_f[0:64, :],
                   vtm_sb[r0:r0 + 64, (t0 // 128) * 512 + h * 128:(t0 // 128) * 512 + (h + 1) * 128])
            kb.wait_dma(V)
            V.wait(ss)
            V.tensor_tensor(t1, kchunk, kchunk, op=ALU.mult)
            T.wait(sv); T.wait(ss)
            T.matmul(kb.ps_c[0:1, 0:64], kb.ones_col, t1, start=True, stop=True).then_inc(smm, 1)
            S.wait(smm)
            S.activation(rrow, kb.ps_c[0:1, 0:64], AF.Ln, bias=kb.col("c_eps6")[0:1, :])
            S.activation(rrow, rrow, AF.Exp, scale=-0.5)
            T.wait(ss)
            T.matmul(kb.ps_c[:, 64:128], kb.ones_row, rrow, start=True, stop=True).then_inc(smm, 1)
            T.matmul(kb.ps_b[:, 0:64], kb.ones_row, brow, start=True, stop=True).then_inc(smm, 1)
            V.wait(smm)
            V.tensor_tensor(knorm, kchunk, kb.ps_c[:, 64:128], op=ALU.mult)
            V.tensor_tensor_scan(cum, onesf, gch, 0.0, op0=ALU.mult, op1=ALU.add)
            S.wait(sv)
            S.activation(P, cum, AF.Exp)
            S.activation(Pm, cum, AF.Exp, scale=-1.0)
            V.wait(ss)
            V.tensor_tensor(t1, gch, cum, op=ALU.subtract)
            S.wait(sv)
            S.activation(Ppm, t1, AF.Exp)
            V.wait(ss)
            V.tensor_tensor(t1, cum, gch, op=ALU.subtract)
            S.wait(sv)
            S.activation(Pprev, t1, AF.Exp)
            V.wait(ss)
            V.tensor_tensor(t1, knorm, kb.ps_b[:, 0:64], op=ALU.mult)
            V.tensor_scalar(t1, t1, -1.0, None, op0=ALU.mult)
            V.tensor_tensor(U_[:, 0:C], t1, Ppm, op=ALU.mult)
            V.tensor_tensor(U_[:, C:J], t1, Pm, op=ALU.mult)
            V.tensor_tensor(Ct_[:, 0:C], knorm, Pprev, op=ALU.mult)
            V.tensor_tensor(Ct_[:, C:J], knorm, P, op=ALU.mult)
            V.tensor_tensor(khat, knorm, Pm, op=ALU.mult)
            V.tensor_tensor(qt, qch, P, op=ALU.mult)
            T.wait(sv)
            T.matmul(psA, Ct_, U_, start=True, stop=True).then_inc(smm, 1)
            T.matmul(psB, U_, Ct_, start=True, stop=True).then_inc(smm, 1)
            V.wait(smm)
            V.tensor_tensor(L_sb, psA, lmask, op=ALU.mult)
            V.tensor_tensor(LT_sb, psB, lmaskT, op=ALU.mult)
            T.wait(sv)
            T.matmul(kb.ps_a[0:64, 128:256], khat, Ct_, start=True, stop=True).then_inc(smm, 1)
            V.wait(smm)
            V.tensor_tensor(LkT_sb[0:64, :], kb.ps_a[0:64, 128:256], kmT[0:64, :], op=ALU.mult)
            T.wait(sv); T.wait(ss)
            T.matmul(psX, Ct_, S_sb, start=True, stop=False)
            T.matmul(psX, LkT_sb[0:64, :], vv_f[0:64, :], start=False, stop=True).then_inc(smm, 1)
            S.wait(smm)
            S.copy(X_sb, psX)
            for i in range(7):
                T.wait(ss); T.wait(sv)
                T.matmul(psA, LT_sb, X_sb, start=True, stop=True).then_inc(smm, 1)
                V.wait(smm)
                V.tensor_tensor(X_sb, X_sb, psA, op=ALU.add)
                if i < 6:
                    T.matmul(psB, LT_sb, L_sb, start=True, stop=True).then_inc(smm, 1)
                    T.matmul(psW, L_sb, LT_sb, start=True, stop=True).then_inc(smm, 1)
                    V.wait(smm)
                    V.tensor_copy(L_sb, psB)
                    V.tensor_copy(LT_sb, psW)
            T.wait(sv); T.wait(ss)
            T.matmul(psA[:, 0:64], U_, qt, start=True, stop=True).then_inc(smm, 1)
            T.matmul(psB[0:64, 0:64], khat, qt, start=True, stop=True).then_inc(smm, 1)
            V.wait(smm)
            V.tensor_tensor(A1T_sb, psA[:, 0:64], qmT, op=ALU.mult)
            V.tensor_tensor(A2T_sb[0:64, :], psB[0:64, 0:64], qkmT[0:64, 0:64], op=ALU.mult)
            T.wait(sv)
            T.matmul(ps_o, qt, S_sb, start=True, stop=False)
            T.matmul(ps_o, A1T_sb, X_sb, start=False, stop=False)
            T.matmul(ps_o, A2T_sb[0:64, :], vv_f[0:64, :], start=False, stop=True).then_inc(smm, 1)
            T.transpose(psW, U_, kb.eye_t[:, :]).then_inc(smm, 1)
            S.wait(smm)
            S.copy(Usm_sb, psW)
            T.wait(ss)
            T.wait(sv)
            T.transpose(psB[0:64, :], khat, kb.eye_t[:, :]).then_inc(smm, 1)
            S.wait(smm)
            S.copy(ksm_sb[0:64, :], psB[0:64, :])
            T.wait(ss)
            T.matmul(psX, Usm_sb, X_sb, start=True, stop=False)
            T.matmul(psX, ksm_sb[0:64, :], vv_f[0:64, :], start=False, stop=True).then_inc(smm, 1)
            V.wait(smm)
            V.tensor_tensor(Sscr, S_sb, psX, op=ALU.add)
            V.tensor_scalar(S_sb, Sscr, P[:, C - 1:C], None, op0=ALU.mult)
            S.wait(smm)
            S.copy(o_tm[0:64, :], ps_o)
            T.wait(ss)
            T.matmul(kb.ps_d[:, 0:64], o_tm[0:64, :], eye64, start=True, stop=True).then_inc(smm, 1)
            S.wait(smm)
            S.copy(o_fm, kb.ps_d[:, 0:64])
            SY.wait(ss)
            SY.dma_start(o_d[h * 128:(h + 1) * 128, t0:t0 + C], o_fm).then_inc(sld, 16)

    for e in (T, S, V, SY, G):
        e.wait(sv); e.wait(ss)
    kb.refresh_banks()
    wo_sb = ar.alloc(BF16, 4096)
    G.dma_start(wo_sb, wo_d[:, :]).then_inc(kb.sem("sldg"), 16)
    ot = ar.alloc(BF16, 512)
    gt = ar.alloc(BF16, 512)
    yf = ar.alloc(F32, 512)
    sq = ar.alloc(F32, 512)
    rr2 = ar.alloc(F32, 512)[0:1, :]
    ybf = [ar.alloc(BF16, 512) for _ in range(4)]
    ptile = ar.alloc(BF16, 512)
    kb.wait_dma(T)
    for nt in range(8):
        n0 = nt * 512
        for m in range(4):
            SY.wait(sv); SY.wait(smm)
            SY.dma_start(ot, o_d[m * 128:(m + 1) * 128, n0:n0 + 512]).then_inc(sld, 16)
            SY.dma_start(gt, g_d[m * 128:(m + 1) * 128, n0:n0 + 512]).then_inc(sld, 16)
            kb.wait_dma(V)
            V.wait(ss)
            V.tensor_tensor(yf, ot, gt, op=ALU.mult)
            V.tensor_tensor(sq, yf, yf, op=ALU.mult)
            T.wait(sv); T.wait(ss)
            T.matmul(kb.ps_c[0:1, 0:512], kb.ones_col, sq, start=True, stop=True).then_inc(smm, 1)
            S.wait(smm)
            S.activation(rr2, kb.ps_c[0:1, 0:512], AF.Ln, bias=kb.col("c_eps6")[0:1, :], scale=1.0 / 128)
            S.activation(rr2, rr2, AF.Exp, scale=-0.5)
            T.wait(ss)
            T.matmul(kb.ps_b[:, 0:512], kb.ones_row, rr2, start=True, stop=True).then_inc(smm, 1)
            V.wait(smm)
            V.tensor_tensor(ybf[m], yf, kb.ps_b[:, 0:512], op=ALU.mult)
        for j in range(4):
            for dn in range(2):
                psv = kb.ps_mm[:, ((2 * j + dn) % 4) * 512: ((2 * j + dn) % 4) * 512 + 512]
                T.wait(sv); T.wait(ss)
                for m in range(4):
                    inst = T.matmul(psv, ybf[m][:, j * 128:(j + 1) * 128],
                                    wo_sb[:, m * 1024 + dn * 512: m * 1024 + dn * 512 + 512],
                                    start=(m == 0), stop=(m == 3))
                inst.then_inc(smm, 1)
                S.wait(smm)
                kb.wait_dma(S)
                S.copy(ptile, psv)
                SY.wait(ss)
                SY.dma_start(part_d[n0 + j * 128: n0 + (j + 1) * 128, dn * 512:(dn + 1) * 512],
                             ptile).then_inc(sld, 16)
    kb.wait_dma(G)
    kb.collective("ReduceScatter", ALU.add, [part_d.ap().opt()], [rs_d.ap().opt()])
    rst = ar.alloc(BF16, TOK)
    xio = ar.alloc(F32, 2 * 512)
    xio2 = ar.alloc(F32, 2 * 512)
    for c in range(DC):
        S.wait(sv)
        S.dma_start(rst, rs_d[:, c * 128:(c + 1) * 128], transpose=True).then_inc(kb.sem("slds"), 16)
        kb.wait_dma(V)
        for n in range(4):
            i = n % 2
            SY.wait(sv)
            SY.dma_start(xio[:, i * 512:(i + 1) * 512],
                         xin[c * 128:(c + 1) * 128, n * 512:(n + 1) * 512]).then_inc(kb.sem("slds"), 16)
            kb.wait_dma(V)
            V.tensor_tensor(xio2[:, i * 512:(i + 1) * 512], xio[:, i * 512:(i + 1) * 512],
                            rst[:, n * 512:(n + 1) * 512], op=ALU.add)
            SY.wait(sv)
            SY.dma_start(xout[c * 128:(c + 1) * 128, n * 512:(n + 1) * 512],
                         xio2[:, i * 512:(i + 1) * 512]).then_inc(sld, 16)
    ar.pop()


def build_l1_ffn(kb, xin, xout):
    hp = kb.hp
    I = hp.I
    G, T, S, V, SY = (kb.e[n] for n in ("gpsimd", "tensor", "scalar", "vector", "sync"))
    smm, sep, sv, ss, sld = (kb.sem(n) for n in ("smm", "sep", "sv", "ss", "sld"))
    DH, WIN, NB = 64, 512, 8
    wqkv = np.asarray(I["L1_ffn_wqkv"], np.float32)
    qs_a = np.asarray(I["L1_ffn_q_scale"], np.float32)
    ks_a = np.asarray(I["L1_ffn_k_scale"], np.float32)
    wo_a = np.asarray(I["L1_ffn_wo"], np.float32)
    for l in range(2):
        qkv_pc, wo_pc = [], []
        for c in range(NCORES):
            h0 = (c % 2) * 4
            qc = slice(h0 * DH, h0 * DH + 256)
            kc_ = slice(512 + h0 * DH, 512 + h0 * DH + 256)
            vc = slice(1024 + h0 * DH, 1024 + h0 * DH + 256)
            qkv_pc.append(_block_w(np.concatenate(
                [wqkv[l][:, qc], wqkv[l][:, kc_], wqkv[l][:, vc]], 1)))
            wb = np.zeros((128, 2048), np.float32)
            for h in range(4):
                wb[(h % 2) * 64:(h % 2) * 64 + 64, (h // 2) * 1024:(h // 2) * 1024 + 1024] = \
                    wo_a[l][(h0 + h) * DH:(h0 + h + 1) * DH, :]
            wo_pc.append(wb)
        hp.add_percore(f"l1qkv{l}", qkv_pc)
        hp.add_percore(f"l1wo{l}", wo_pc)
        hp.add_shared(f"l1w1a{l}", prep_w(np.asarray(I["L1_ffn_w1"], np.float32)[l][:, 0:2730], 1024, 2816))
        hp.add_shared(f"l1w1g{l}", prep_w(np.asarray(I["L1_ffn_w1"], np.float32)[l][:, 2730:5460], 1024, 2816))
        hp.add_shared(f"l1w2{l}", prep_w(np.asarray(I["L1_ffn_w2"], np.float32)[l], 2816, 1024))
        hp.add_cols(f"l1b1a{l}", np.asarray(I["L1_ffn_b1"], np.float32)[l][0:2730], pad_to=2816)
        hp.add_cols(f"l1b1g{l}", np.asarray(I["L1_ffn_b1"], np.float32)[l][2730:5460], pad_to=2816)
        hp.add_cols(f"l1b2{l}", np.asarray(I["L1_ffn_b2"], np.float32)[l])
        hp.add_cols(f"l1g{l}", np.asarray(I["L1_ffn_ln_g"], np.float32)[l])
        hp.add_cols(f"l1b{l}", np.asarray(I["L1_ffn_ln_b"], np.float32)[l])
        hp.add_col(f"l1qs{l}", np.tile(qs_a[l] * 8.0, 2))
        hp.add_col(f"l1ks{l}", np.tile(ks_a[l], 2))
    hp.add_percore_col("l1fl", [np.full(128, 1.0 if c % 2 == 1 else 0.0, np.float32)
                                for c in range(NCORES)])
    hp.add_col("c_eps24", np.full(128, 1e-24, np.float32))
    half = np.arange(0, DH, 2, dtype=np.float64)
    inv_freq = 1.0 / 10000.0 ** (half / DH)
    t_ar = np.arange(TFULL, dtype=np.float64)
    freqs = t_ar[:, None] * inv_freq[None, :]
    freqs = np.concatenate([freqs, freqs], -1)
    base = (half + 0.4 * DH) / (1.4 * DH)
    power = (t_ar - TFULL // 2) / float(WIN // 2)
    sc = base[None, :] ** power[:, None]
    sc = np.concatenate([sc, sc], -1)
    cosf, sinf = np.cos(freqs), np.sin(freqs)
    for nm, tab in (("cq", cosf * sc), ("sq", sinf * sc), ("ck", cosf / sc), ("sk", sinf / sc)):
        hp.add_shared(f"l1t_{nm}", np.ascontiguousarray(np.tile(tab.T, (2, 1))).astype(BF))
    Mr = np.zeros((128, 128), np.float32)
    for blk in range(2):
        o = blk * 64
        for i in range(32):
            Mr[o + i, o + 32 + i] = -1.0
            Mr[o + 32 + i, o + i] = 1.0
    hp.add_shared("l1_rotT", np.ascontiguousarray(Mr.T).astype(BF))
    hp.add_shared("l1_blk", np.stack([np.repeat([1.0, 0.0], 64), np.repeat([0.0, 1.0], 64)], 1).astype(np.float32))
    hp.add_shared("l1_rsel", np.ascontiguousarray(np.stack([np.repeat([1.0, 0.0], 64), np.repeat([0.0, 1.0], 64)], 1).T))
    iidx = np.arange(WIN)
    jc = np.arange(2 * WIN)
    jrel = jc[None, :] - WIN
    valid = (iidx[:, None] >= jrel) & (iidx[:, None] <= jrel + WIN)
    mq0 = np.where(valid, 0.0, -30000.0).astype(np.float32)
    mqb0 = np.where(valid & (jc[None, :] >= WIN), 0.0, -30000.0).astype(np.float32)
    for var, mm_ in ((0, mq0), (1, mqb0)):
        for qt in range(4):
            hp.add_shared(f"l1mq{var}_{qt}", mm_[qt * 128:(qt + 1) * 128, :].astype(BF))
        for kc2 in range(8):
            hp.add_shared(f"l1mk{var}_{kc2}",
                          np.ascontiguousarray(mm_[:, kc2 * 128:(kc2 + 1) * 128].T).astype(BF))
    qkv_d = [kb.dram_t(f"l1qkv{l}", [128, 1024 * 768 // 128], F32, kind="ExternalInput") for l in range(2)]
    wo_d = [kb.dram_t(f"l1wo{l}", [128, 2048], F32, kind="ExternalInput") for l in range(2)]
    w1a_d = [kb.dram_t(f"l1w1a{l}", [128, 1024 * 2816 // 128], F32, kind="ExternalInput") for l in range(2)]
    w1g_d = [kb.dram_t(f"l1w1g{l}", [128, 1024 * 2816 // 128], F32, kind="ExternalInput") for l in range(2)]
    w2_d = [kb.dram_t(f"l1w2{l}", [128, 2816 * 1024 // 128], F32, kind="ExternalInput") for l in range(2)]
    tab_d = {nm: kb.dram_t(f"l1t_{nm}", [128, TFULL], BF16, kind="ExternalInput")
             for nm in ("cq", "sq", "ck", "sk")}
    rotT_d = kb.dram_t("l1_rotT", [128, 128], BF16, kind="ExternalInput")
    blk_d = kb.dram_t("l1_blk", [128, 2], F32, kind="ExternalInput")
    rsel_d = kb.dram_t("l1_rsel", [2, 128], F32, kind="ExternalInput")
    mq_d = {(v, qt): kb.dram_t(f"l1mq{v}_{qt}", [128, 1024], BF16, kind="ExternalInput")
            for v in range(2) for qt in range(4)}
    mk_d = {(v, kc2): kb.dram_t(f"l1mk{v}_{kc2}", [128, 512], BF16, kind="ExternalInput")
            for v in range(2) for kc2 in range(8)}
    s_d = kb.dram_t("l1s", [1024, TOK], F32)
    sbf_d = kb.dram_t("l1sbf", [1024, TOK], BF16)
    sag_d = kb.dram_t("l1sag", [2048, TOK], BF16)
    ht_d = kb.dram_t("l1ht", [1024, TFULL], BF16)
    q_d = kb.dram_t("l1q", [256, TFULL], BF16)
    k_d = kb.dram_t("l1k", [256, TFULL], BF16)
    v_d = kb.dram_t("l1v", [256, TFULL], BF16)
    part_d = kb.dram_t("l1part2", [TFULL, 1024], BF16)
    rs_d = kb.dram_t("l1rs2", [TOK, 1024], BF16)
    tail_d = kb.dram_t("l1tail", [512, 16], BF16)
    tailag_d = kb.dram_t("l1tailag", [1024, 16], BF16)

    CUT = int(os.environ.get("L1CUT", "0"))
    ar = kb.ar
    ar.push()

    def stream_sbf():
        ar.push()
        tb = ar.alloc(F32, 512)
        tb2 = ar.alloc(BF16, 512)
        for c in range(DC):
            for n in range(4):
                SY.wait(sv)
                SY.dma_start(tb, s_d[c * 128:(c + 1) * 128, n * 512:(n + 1) * 512]).then_inc(sld, 16)
                kb.wait_dma(V)
                V.tensor_copy(tb2, tb)
                SY.wait(sv)
                SY.dma_start(sbf_d[c * 128:(c + 1) * 128, n * 512:(n + 1) * 512], tb2).then_inc(sld, 16)
        ar.pop()

    ar.push()
    uf = ar.alloc(F32, DC * TOK)
    xs_buf = ar.alloc(F32, DC * TOK)
    xsv = emit_load_x(kb, xin, xs_buf)
    emit_ln(kb, xsv, lambda c: uf[:, c * TOK:(c + 1) * TOK], TOK)
    ub = ar.alloc(BF16, DC * TOK)
    V.wait(sv)
    for c in range(DC):
        V.tensor_copy(ub[:, c * TOK:(c + 1) * TOK], uf[:, c * TOK:(c + 1) * TOK])
    G.wait(sv)
    for c in range(DC):
        G.dma_start(s_d[c * 128:(c + 1) * 128, :], uf[:, c * TOK:(c + 1) * TOK]).then_inc(kb.sem("sldg"), 16)
        G.dma_start(sbf_d[c * 128:(c + 1) * 128, :], ub[:, c * TOK:(c + 1) * TOK]).then_inc(kb.sem("sldg"), 16)
    kb.wait_dma(G)
    ar.pop()

    for l in ([0] if CUT else range(2)):
        kb.wait_dma(G)
        kb.collective("AllGather", ALU.bypass, [sbf_d.ap().opt()], [sag_d.ap().opt()])
        for c in range(4):
            SY.dma_start(ht_d[c * 128:(c + 1) * 128, 0:TOK], sag_d[c * 128:(c + 1) * 128, :]).then_inc(sld, 16)
            SY.dma_start(ht_d[c * 128:(c + 1) * 128, TOK:TFULL],
                         sag_d[1024 + c * 128:1024 + (c + 1) * 128, :]).then_inc(sld, 16)
        for c in range(4, 8):
            SY.dma_start(ht_d[c * 128:(c + 1) * 128, 0:1], kb.zt_bf[:, 0:1],
                         allow_slow_non_contiguous=True).then_inc(sld, 16)
            SY.dma_start(ht_d[c * 128:(c + 1) * 128, 1:TOK + 1],
                         sag_d[c * 128:(c + 1) * 128, 0:TOK]).then_inc(sld, 16)
            SY.dma_start(ht_d[c * 128:(c + 1) * 128, TOK + 1:TFULL],
                         sag_d[1024 + c * 128:1024 + (c + 1) * 128, 0:TOK - 1]).then_inc(sld, 16)
        ar.push()
        tabs = {nm: ar.alloc(BF16, TFULL) for nm in ("cq", "sq", "ck", "sk")}
        for nm in tabs:
            SY.dma_start(tabs[nm], tab_d[nm][:, :]).then_inc(sld, 16)
        rotT = ar.alloc(BF16, 128)
        blk_sb = ar.alloc(F32, 2)
        rsel_sb = ar.alloc(F32, 128)
        SY.dma_start(rotT, rotT_d[:, :]).then_inc(sld, 16)
        SY.dma_start(blk_sb, blk_d[:, :]).then_inc(sld, 16)
        SY.dma_start(rsel_sb[0:2, :], rsel_d[:, :]).then_inc(sld, 16)
        mq_sb = {}
        mk_sb = {}
        for v in range(2):
            for qt in range(4):
                mq_sb[(v, qt)] = ar.alloc(BF16, 1024)
                SY.dma_start(mq_sb[(v, qt)], mq_d[(v, qt)][:, :]).then_inc(sld, 16)
            for kc2 in range(8):
                mk_sb[(v, kc2)] = ar.alloc(BF16, 512)
                SY.dma_start(mk_sb[(v, kc2)], mk_d[(v, kc2)][:, :]).then_inc(sld, 16)
        sqb = ar.alloc(F32, 512)
        rr2 = ar.alloc(F32, 512)
        rbc = ar.alloc(F32, 512)
        qn = ar.alloc(F32, 512)
        qnb = ar.alloc(BF16, 512)
        ta_f = ar.alloc(F32, 512)
        tb_f = ar.alloc(F32, 512)
        qxb = ar.alloc(BF16, 512)
        vst = ar.alloc(BF16, 512)

        def rhs_tile(kc, n0):
            return ht_d[kc * 128:(kc + 1) * 128, n0:n0 + 512]

        def epi_qkv(m, n0, nn, psv, _l=l):
            if m >= 4:
                S.wait(smm)
                kb.wait_dma(S)
                S.activation(vst[:, 0:nn], psv, AF.Copy)
                SY.wait(ss)
                SY.dma_start(v_d[(m - 4) * 128:(m - 3) * 128, n0:n0 + nn], vst[:, 0:nn]).then_inc(sld, 16)
                return
            isq = m < 2
            dst = q_d if isq else k_d
            mm_ = m if isq else m - 2
            ct, st = (tabs["cq"], tabs["sq"]) if isq else (tabs["ck"], tabs["sk"])
            scol = kb.col(f"l1qs{_l}") if isq else kb.col(f"l1ks{_l}")
            S.wait(smm)
            kb.wait_dma(S)
            S.activation(sqb[:, 0:nn], psv, AF.Square)
            T.wait(ss)
            T.matmul(kb.ps_c[0:2, 0:nn], blk_sb, sqb[:, 0:nn], start=True, stop=True).then_inc(smm, 1)
            S.wait(smm)
            S.activation(rr2[0:2, 0:nn], kb.ps_c[0:2, 0:nn], AF.Ln, bias=kb.col("c_eps24")[0:2, :])
            S.activation(rr2[0:2, 0:nn], rr2[0:2, 0:nn], AF.Exp, scale=-0.5)
            T.wait(ss)
            T.matmul(kb.ps_d[:, 0:nn], rsel_sb[0:2, :], rr2[0:2, 0:nn], start=True, stop=True).then_inc(smm, 1)
            S.wait(smm)
            S.copy(rbc[:, 0:nn], kb.ps_d[:, 0:nn])
            V.wait(ss)
            V.tensor_tensor(qn[:, 0:nn], psv, rbc[:, 0:nn], op=ALU.mult)
            V.tensor_scalar(qnb[:, 0:nn], qn[:, 0:nn], scol, None, op0=ALU.mult)
            T.wait(sv); T.wait(ss)
            T.matmul(kb.ps_d[:, 0:nn], rotT, qnb[:, 0:nn], start=True, stop=True).then_inc(smm, 1)
            V.tensor_tensor(ta_f[:, 0:nn], qnb[:, 0:nn], ct[:, n0:n0 + nn], op=ALU.mult)
            V.wait(smm)
            V.tensor_tensor(tb_f[:, 0:nn], kb.ps_d[:, 0:nn], st[:, n0:n0 + nn], op=ALU.mult)
            V.tensor_tensor(qxb[:, 0:nn], ta_f[:, 0:nn], tb_f[:, 0:nn], op=ALU.add)
            SY.wait(sv)
            SY.dma_start(dst[mm_ * 128:(mm_ + 1) * 128, n0:n0 + nn], qxb[:, 0:nn]).then_inc(sld, 16)
        emit_linear_dramrhs(kb, qkv_d[l].ap(), 1024, 768, rhs_tile, TFULL, epi_qkv)
        if CUT == 3:
            ar.pop()
            break

        kb.refresh_banks()
        k_sb = ar.alloc(BF16, 2048)
        q_sb = ar.alloc(BF16, 1024)
        vtm = ar.alloc(BF16, 2048)
        tmsk = ar.alloc(F32, 1024)
        esb = ar.alloc(F32, 512)
        ekm = ar.alloc(BF16, 512)
        mneg = ar.alloc(F32, 4)
        msb = ar.alloc(F32, 128)
        mrow = ar.alloc(F32, 512)[0:1, :]
        rrow = ar.alloc(F32, 512)[0:1, :]
        rbc2 = ar.alloc(F32, 512)
        obuf = ar.alloc(BF16, 1024)
        wo_sb = ar.alloc(BF16, 2048)
        ptile = ar.alloc(BF16, 512)
        G.wait(sv)
        G.dma_start(wo_sb, wo_d[l][:, :]).then_inc(kb.sem("sldg"), 16)
        for e in (T, S, V, SY):
            kb.wait_dma(e)
        for b in range(NB):
            var = 1 if b == 0 else 0
            kb0 = max(0, (b - 1) * 512)
            SY.wait(sv); SY.wait(ss); SY.wait(smm)
            for c2 in range(2):
                if b == 0:
                    SY.dma_start(k_sb[:, c2 * 1024:c2 * 1024 + 512],
                                 k_d[c2 * 128:(c2 + 1) * 128, 0:512]).then_inc(sld, 16)
                    SY.dma_start(k_sb[:, c2 * 1024 + 512:c2 * 1024 + 1024],
                                 k_d[c2 * 128:(c2 + 1) * 128, 0:512]).then_inc(sld, 16)
                else:
                    SY.dma_start(k_sb[:, c2 * 1024:(c2 + 1) * 1024],
                                 k_d[c2 * 128:(c2 + 1) * 128, kb0:kb0 + 1024]).then_inc(sld, 16)
                SY.dma_start(q_sb[:, c2 * 512:(c2 + 1) * 512],
                             q_d[c2 * 128:(c2 + 1) * 128, b * 512:(b + 1) * 512]).then_inc(sld, 16)
            for tc in range(8):
                t0v = kb0 + tc * 128 if b > 0 else (tc % 4) * 128
                S.dma_start(vtm[:, tc * 256:(tc + 1) * 256], v_d[:, t0v:t0v + 128],
                            transpose=True).then_inc(kb.sem("slds"), 16)
            for e in (T, V, S):
                kb.wait_dma(e)
            for h in range(4):
                base = (h % 2) * 64
                c2 = h // 2
                for qt in range(4):
                    T.wait(sv); T.wait(ss)
                    for dn in range(2):
                        inst = T.matmul(kb.ps_mm[:, dn * 512:(dn + 1) * 512],
                                        q_sb[base:base + 64, c2 * 512 + qt * 128: c2 * 512 + qt * 128 + 128],
                                        k_sb[base:base + 64, c2 * 1024 + dn * 512: c2 * 1024 + (dn + 1) * 512],
                                        start=True, stop=True)
                    inst.then_inc(smm, 1)
                    V.wait(smm)
                    V.tensor_tensor(tmsk[:, 0:512], kb.ps_mm[:, 0:512],
                                    mq_sb[(var, qt)][:, 0:512], op=ALU.add)
                    V.tensor_tensor(tmsk[:, 512:1024], kb.ps_mm[:, 512:1024],
                                    mq_sb[(var, qt)][:, 512:1024], op=ALU.add)
                    V.tensor_reduce(mneg[:, qt:qt + 1], tmsk, axis=mybir.AxisListType.X,
                                    op=ALU.max, negate=True)
                T.wait(sv)
                T.transpose(kb.ps_d[0:4, 0:128], mneg, kb.eye_t[:, :]).then_inc(smm, 1)
                S.wait(smm)
                S.copy(msb[0:4, :], kb.ps_d[0:4, 0:128])
                G.wait(ss)
                G.wait(smm)
                for qt in range(4):
                    G.dma_start(mrow[0:1, qt * 128:(qt + 1) * 128],
                                msb[qt:qt + 1, 0:128]).then_inc(kb.sem("sldg"), 16)
                kb.wait_dma(T)
                if CUT == 5:
                    continue
                for kc2 in range(8):
                    T.wait(sv); T.wait(ss)
                    psv = kb.ps_mm[:, 1024:1536]
                    T.matmul(psv, k_sb[base:base + 64, c2 * 1024 + kc2 * 128: c2 * 1024 + kc2 * 128 + 128],
                             q_sb[base:base + 64, c2 * 512:(c2 + 1) * 512], start=True, stop=False)
                    T.matmul(psv, kb.ones_row[0:1, 0:128],
                             mrow, start=False, stop=True).then_inc(smm, 1)
                    V.wait(smm)
                    V.wait(ss)
                    V.tensor_tensor(esb, psv, mk_sb[(var, kc2)], op=ALU.add)
                    S.wait(sv)
                    S.wait(smm)
                    S.activation(ekm, esb, AF.Exp)
                    T.wait(ss)
                    T.matmul(kb.ps_a[0:1, 0:512], kb.ones_col_bf, ekm,
                             start=(kc2 == 0), stop=(kc2 == 7))
                    T.matmul(kb.ps_b[0:64, 0:512], vtm[:, kc2 * 256 + h * 64: kc2 * 256 + (h + 1) * 64],
                             ekm, start=(kc2 == 0), stop=(kc2 == 7)).then_inc(smm, 1)
                V.wait(smm)
                V.reciprocal(rrow, kb.ps_a[0:1, 0:512])
                T.wait(sv)
                T.matmul(kb.ps_c[0:64, 0:512], kb.ones_row[0:1, 0:64], rrow,
                         start=True, stop=True).then_inc(smm, 1)
                S.wait(smm)
                S.copy(rbc2[0:64, :], kb.ps_c[0:64, 0:512])
                V.wait(ss)
                V.tensor_tensor(obuf[base:base + 64, c2 * 512:(c2 + 1) * 512],
                                kb.ps_b[0:64, 0:512], rbc2[0:64, :], op=ALU.mult)
                T.wait(sv)
            for tt in (range(0) if CUT in (5, 6) else range(4)):
                for dn in range(2):
                    psv = kb.ps_mm[:, ((2 * tt + dn) % 4) * 512: ((2 * tt + dn) % 4) * 512 + 512]
                    T.wait(sv); T.wait(ss)
                    for c2 in range(2):
                        inst = T.matmul(psv, obuf[:, c2 * 512 + tt * 128: c2 * 512 + tt * 128 + 128],
                                        wo_sb[:, c2 * 1024 + dn * 512: c2 * 1024 + (dn + 1) * 512],
                                        start=(c2 == 0), stop=(c2 == 1))
                    inst.then_inc(smm, 1)
                    S.wait(smm)
                    kb.wait_dma(S)
                    S.copy(ptile, psv)
                    SY.wait(ss)
                    SY.dma_start(part_d[b * 512 + tt * 128: b * 512 + (tt + 1) * 128,
                                        dn * 512:(dn + 1) * 512], ptile).then_inc(sld, 16)
        if CUT == 4:
            ar.pop()
            break
        kb.wait_dma(G)
        kb.collective("ReduceScatter", ALU.add, [part_d.ap().opt()], [rs_d.ap().opt()])
        rst = ar.alloc(BF16, TOK)
        xio = ar.alloc(F32, 2 * 512)
        xio2 = ar.alloc(F32, 2 * 512)
        for c in range(DC):
            S.wait(sv)
            S.dma_start(rst, rs_d[:, c * 128:(c + 1) * 128], transpose=True).then_inc(kb.sem("slds"), 16)
            kb.wait_dma(V)
            for n in range(4):
                i = n % 2
                SY.wait(sv)
                SY.dma_start(xio[:, i * 512:(i + 1) * 512],
                             s_d[c * 128:(c + 1) * 128, n * 512:(n + 1) * 512]).then_inc(kb.sem("slds"), 16)
                kb.wait_dma(V)
                V.tensor_tensor(xio2[:, i * 512:(i + 1) * 512], xio[:, i * 512:(i + 1) * 512],
                                rst[:, n * 512:(n + 1) * 512], op=ALU.add)
                SY.wait(sv)
                SY.dma_start(s_d[c * 128:(c + 1) * 128, n * 512:(n + 1) * 512],
                             xio2[:, i * 512:(i + 1) * 512]).then_inc(sld, 16)
        ar.pop()
        if CUT == 1:
            break
        ar.push()
        tcol = ar.alloc(F32, 16)
        tcolb = ar.alloc(BF16, 16)
        for c in range(4):
            SY.wait(sv)
            SY.dma_start(tcol[:, 0:1], s_d[512 + c * 128: 512 + (c + 1) * 128, TOK - 1:TOK],
                         allow_slow_non_contiguous=True).then_inc(sld, 16)
            kb.wait_dma(V)
            V.tensor_copy(tcolb[:, 0:1], tcol[:, 0:1])
            V.memset(tcolb[:, 1:16], 0.0)
            SY.wait(sv)
            SY.dma_start(tail_d[c * 128:(c + 1) * 128, :], tcolb).then_inc(sld, 16)
        kb.wait_dma(SY)
        kb.wait_dma(G)
        kb.collective("AllGather", ALU.bypass, [tail_d.ap().opt()], [tailag_d.ap().opt()])
        uln = ar.alloc(BF16, DC * TOK)
        ar.push()
        xs2 = ar.alloc(F32, DC * TOK)
        for c in range(4):
            SY.dma_start(xs2[:, c * TOK:(c + 1) * TOK],
                         s_d[c * 128:(c + 1) * 128, :]).then_inc(sld, 16)
        for c in range(4, 8):
            SY.dma_start(xs2[:, c * TOK + 1:(c + 1) * TOK],
                         s_d[c * 128:(c + 1) * 128, 0:TOK - 1]).then_inc(sld, 16)
            G.wait(sv)
            G.dma_start(xs2[:, c * TOK:c * TOK + 1],
                        tailag_d[(c - 4) * 128:(c - 3) * 128, 0:1],
                        allow_slow_non_contiguous=True).then_inc(kb.sem("sldg"), 16)
        kb.wait_dma(V)
        for c in range(4, 8):
            V.tensor_scalar(xs2[:, c * TOK:c * TOK + 1], xs2[:, c * TOK:c * TOK + 1],
                            kb.col("l1fl"), None, op0=ALU.mult)
        for e in (S, T, V):
            kb.wait_dma(e)
        T.wait(sv)
        S.wait(sv)
        emit_ln(kb, lambda c: xs2[:, c * TOK:(c + 1) * TOK],
                lambda c: uln[:, c * TOK:(c + 1) * TOK], TOK,
                g_col=lambda c, _l=l: kb.col(f"l1g{_l}_{c}"),
                b_col=lambda c, _l=l: kb.col(f"l1b{_l}_{c}"))
        ar.pop()
        gbuf = ar.alloc(BF16, 22 * TOK)
        T.wait(sv)
        uview = lambda kc, n0, nn: uln[:, kc * TOK + n0: kc * TOK + n0 + nn]
        emit_linear(kb, w1g_d[l].ap(), 1024, 2816, uview, TOK,
                    epi_act(kb, AF.Gelu, lambda m, n0, nn: gbuf[:, m * TOK + n0: m * TOK + n0 + nn],
                            bias_fn=lambda m, _l=l: kb.col(f"l1b1g{_l}_{m}")))
        aband = ar.alloc(F32, 512)

        def epi_a(m, n0, nn, psv, _l=l):
            V.wait(smm)
            V.wait(ss)
            V.tensor_scalar(aband[:, 0:nn], psv, kb.col(f"l1b1a{_l}_{m}"), None, op0=ALU.add)
            gsl = gbuf[:, m * TOK + n0: m * TOK + n0 + nn]
            V.tensor_tensor(gsl, gsl, aband[:, 0:nn], op=ALU.mult)
        T.wait(ss)
        emit_linear(kb, w1a_d[l].ap(), 1024, 2816, uview, TOK, epi_a)
        gview = lambda kc, n0, nn: gbuf[:, kc * TOK + n0: kc * TOK + n0 + nn]
        T.wait(sv)
        r_xold = ar.alloc(F32, 2 * 512)
        r_xnew = ar.alloc(F32, 2 * 512)
        r_tb = ar.alloc(F32, 2 * 512)
        r_state = {"i": 0}

        def epi_w2(m, n0, nn, psv, _l=l):
            i = r_state["i"] % 2
            r_state["i"] += 1
            xo = r_xold[:, i * 512: i * 512 + nn]
            xn = r_xnew[:, i * 512: i * 512 + nn]
            tb = r_tb[:, i * 512: i * 512 + nn]
            SY.wait(sv)
            SY.dma_start(xo, s_d[m * 128:(m + 1) * 128, n0:n0 + nn]).then_inc(sld, 16)
            V.wait(smm)
            V.tensor_scalar(tb, psv, kb.col(f"l1b2{_l}_{m}"), None, op0=ALU.add)
            kb.wait_dma(V)
            V.tensor_tensor(xn, tb, xo, op=ALU.add)
            SY.wait(sv)
            SY.dma_start(s_d[m * 128:(m + 1) * 128, n0:n0 + nn], xn).then_inc(sld, 16)
        emit_linear(kb, w2_d[l].ap(), 2816, 1024, gview, TOK, epi_w2)
        if l == 0:
            stream_sbf()
        ar.pop()
    xi1 = ar.alloc(F32, 512)
    xsl = ar.alloc(F32, 512)
    xo1 = ar.alloc(F32, 512)
    for c in range(DC):
        for n in range(4):
            SY.wait(sv)
            SY.dma_start(xi1, xin[c * 128:(c + 1) * 128, n * 512:(n + 1) * 512]).then_inc(sld, 16)
            SY.dma_start(xsl, s_d[c * 128:(c + 1) * 128, n * 512:(n + 1) * 512]).then_inc(sld, 16)
            kb.wait_dma(V)
            V.scalar_tensor_tensor(xo1, xsl, 0.5, xi1, op0=ALU.mult, op1=ALU.add)
            SY.wait(sv)
            SY.dma_start(xout[c * 128:(c + 1) * 128, n * 512:(n + 1) * 512], xo1).then_inc(sld, 16)
    ar.pop()
